# version 10
# baseline (speedup 1.0000x reference)
import numpy as np
import ml_dtypes
from contextlib import ExitStack

import concourse.bass as bass
import concourse.mybir as mybir
from concourse.bass_utils import run_bass_kernel_spmd

F32 = mybir.dt.float32
BF16 = mybir.dt.bfloat16
I32 = mybir.dt.int32
AF = mybir.ActivationFunctionType
ALU = mybir.AluOpType
AX = mybir.AxisListType

T = 4096
D = 2048
HN = 8
DH = 128
II = 8192
TOK = 2048
GAMMA = 1.0 / float(np.sqrt(128.0))
EPS = 1e-5

_DEBUG_TAPS = []


class Sem:
    def __init__(self, h):
        self.h = h
        self.n = 0

    def inc(self, k=1):
        self.n += k
        return (self.h, k)

    def at(self):
        return (self.h, self.n)


class Plan:
    def __init__(self):
        self.items = []

    def op(self, key, fn, waits=(), inc=None):
        flat = []
        for w in waits:
            if w is None:
                continue
            if isinstance(w, list):
                flat.extend(x for x in w if x is not None)
            else:
                flat.append(w)
        self.items.append((key, fn, tuple(flat), inc))

    def replay(self, key, eng):
        seen = {}
        for k, fn, waits, inc in self.items:
            if k != key:
                continue
            for (h, v) in waits:
                hid = id(h)
                if seen.get(hid, -1) < v:
                    eng.wait_ge(h, v)
                    seen[hid] = v
            r = fn(eng)
            if inc is not None:
                r.then_inc(*inc)


class DmaQ:

    def __init__(self, plan, key, sems):
        self.P = plan
        self.key = key
        self.sems = sems
        self.g = -1
        self.first = False
        self.gate = None

    def group(self):
        self.g += 1
        s = self.sems[self.g % len(self.sems)]
        self.gate = (s.h, s.n)
        self.first = True

    def dma(self, fn, waits=()):
        s = self.sems[self.g % len(self.sems)]
        w = list(waits)
        if self.first:
            w.append(self.gate)
            self.first = False
        self.P.op(self.key, fn, waits=w, inc=s.inc(16))

    def done(self):
        s = self.sems[self.g % len(self.sems)]
        return (s.h, s.n)

    def all_done(self):
        return [(s.h, s.n) for s in self.sems]


def build_nc():
    nc = bass.Bass("TRN2", target_bir_lowering=False)

    d_x = nc.dram_tensor("x", [T, D], F32, kind="ExternalInput")
    d_wq = nc.dram_tensor("wq", [HN, 128, 2048], F32, kind="ExternalInput")
    d_wk = nc.dram_tensor("wk", [HN, 128, 2048], F32, kind="ExternalInput")
    d_wv = nc.dram_tensor("wv", [128, 16 * 1024], F32, kind="ExternalInput")
    d_wo = nc.dram_tensor("wo", [128, HN * 2048], F32, kind="ExternalInput")
    d_bqk = nc.dram_tensor("bqk", [128, 16], F32, kind="ExternalInput")
    d_bvrow = nc.dram_tensor("bvrow", [1, HN * DH], F32, kind="ExternalInput")
    d_borow = nc.dram_tensor("borow", [1, D], F32, kind="ExternalInput")
    d_wg = nc.dram_tensor("wg", [64, 128, 2048], F32, kind="ExternalInput")
    d_wu = nc.dram_tensor("wu", [64, 128, 2048], F32, kind="ExternalInput")
    d_wd = nc.dram_tensor("wd", [64, 128, 2048], F32, kind="ExternalInput")
    d_cs = nc.dram_tensor("ctab", [128, T], F32, kind="ExternalInput")
    d_sn = nc.dram_tensor("stab", [128, T], F32, kind="ExternalInput")
    d_mask = nc.dram_tensor("mask_t", [128, 2048], BF16, kind="ExternalInput")
    d_ident = nc.dram_tensor("ident", [128, 128], F32, kind="ExternalInput")
    d_swap = nc.dram_tensor("swapm", [128, 128], BF16, kind="ExternalInput")
    d_own0 = nc.dram_tensor("own0", [1, 2], I32, kind="ExternalInput")
    d_xo = nc.dram_tensor("x_own", [TOK, D], F32, kind="ExternalInput")

    d_out = nc.dram_tensor("out", [TOK, D], F32, kind="ExternalOutput")

    s_xn = nc.dram_tensor("s_xn", [T, D], BF16, kind="Internal")
    s_xnh = nc.dram_tensor("s_xnh", [TOK, D], BF16, kind="Internal")
    s_kT = nc.dram_tensor("s_kT", [HN, 128, T], BF16, kind="Internal")
    s_qT = nc.dram_tensor("s_qT", [HN, 128, T], BF16, kind="Internal")
    s_v = nc.dram_tensor("s_v", [T, HN * DH], BF16, kind="Internal")
    s_wd = nc.dram_tensor("s_wd", [64, 128, 2048], BF16, kind="Internal")
    s_hsp = nc.dram_tensor("s_hsp", [II, TOK], BF16, kind="Internal")
    s_ap = nc.dram_tensor("s_ap", [T, D], BF16, kind="Internal")
    s_ar = nc.dram_tensor("s_ar", [TOK, D], BF16, kind="Internal")
    s_ffn = nc.dram_tensor("s_ffn", [TOK, D], BF16, kind="Internal")

    taps = {}
    tap_shapes = {"xn": [T, D], "kT": [HN, 128, T], "q": [HN, 128, T], "v": [T, HN * DH],
                  "z": [128, HN * T], "ap": [T, D], "hsp": [II, TOK], "ffn": [TOK, D]}
    for name in _DEBUG_TAPS:
        taps[name] = nc.dram_tensor("tap_" + name, tap_shapes[name], BF16, kind="ExternalOutput")

    es = ExitStack()
    sb = lambda n, s, d: es.enter_context(nc.sbuf_tensor(n, s, d))
    pt = lambda n, s, d: es.enter_context(nc.psum_tensor(n, s, d))

    misc = sb("misc", [128, 1536], F32)
    ones_bc = sb("ones_bc", [128, 1], BF16)
    ones_br = sb("ones_br", [1, 128], BF16)
    ones_fr = sb("ones_fr", [1, 128], F32)
    ones_fc = sb("ones_fc", [128, 1], F32)
    ident = sb("identsb", [128, 128], F32)
    swapm = sb("swapsb", [128, 128], BF16)
    bqk = sb("bqksb", [128, 16], F32)
    bvrow = sb("bvrowsb", [1, HN * DH], BF16)
    borow = sb("borowsb", [1, D], BF16)
    kbarb_s = [sb("kbarbsb0", [128, 1], BF16), sb("kbarbsb1", [128, 1], BF16)]
    own0sb = sb("own0sb", [1, 2], I32)


    mc_sums = misc[0:1, 0:512]
    mc_recip = misc[0:1, 512:1024]
    mc_nk = misc[:, 1024:1056]
    mc_dk = misc[:, 1056:1088]
    mc_nq = misc[:, 1088:1120]
    mc_dq = misc[:, 1120:1152]
    mc_r2 = misc[:, 1152:1184]
    mc_qn = misc[:, 1184:1216]
    mc_bq = misc[:, 1216:1248]
    mc_ksum_s = [misc[:, 1248:1249], misc[:, 1260:1261]]
    mc_kbarsq_s = [misc[:, 1249:1250], misc[:, 1261:1262]]
    mc_ksb = misc[:, 1250:1251]
    mc_mr = misc[:, 1251:1252]
    mc_negB_s = [misc[:, 1252:1253], misc[:, 1262:1263]]
    mc_rmax = misc[:, 1253:1254]
    mc_t11a = misc[0:1, 1254:1255]
    mc_t11b = misc[0:1, 1255:1256]

    def ln_slot(i):
        base = 1280 + (i % 4) * 8
        return misc[:, base:base + 8]

    ABYTES = 192 * 1024
    AB = sb("arena", [128, ABYTES // 2], BF16)

    def c_bf(off_b, n, p0=0, p1=128):
        assert off_b % 2 == 0 and off_b + 2 * n <= ABYTES, (off_b, n)
        return AB[p0:p1, off_b // 2: off_b // 2 + n]

    def c_f32(off_b, n, p0=0, p1=128):
        assert off_b % 4 == 0 and off_b + 4 * n <= ABYTES, (off_b, n)
        return AB[p0:p1, off_b // 2: off_b // 2 + 2 * n].bitcast(F32)

    pA = pt("pA", [128, 1024], F32)
    pB = pt("pB", [128, 1024], F32)
    p4 = pt("p4", [128, 512], F32)
    p5 = pt("p5", [128, 512], F32)
    p6 = pt("p6", [128, 512], F32)
    p7 = pt("p7", [128, 512], F32)

    sP = Sem(es.enter_context(nc.semaphore("sP")))
    sA = Sem(es.enter_context(nc.semaphore("sA")))
    sV = Sem(es.enter_context(nc.semaphore("sV")))
    sG = Sem(es.enter_context(nc.semaphore("sG")))
    sC = Sem(es.enter_context(nc.semaphore("sC")))

    P = Plan()
    free = {}

    qS = DmaQ(P, "sync", [Sem(es.enter_context(nc.semaphore(f"qS{i}"))) for i in range(4)])
    qA = DmaQ(P, "scalar", [Sem(es.enter_context(nc.semaphore(f"qA{i}"))) for i in range(4)])
    qG = DmaQ(P, "gpsimd", [Sem(es.enter_context(nc.semaphore(f"qG{i}"))) for i in range(4)])

    P.op("vector", lambda e: e.memset(ones_bc[:, :], 1.0))
    P.op("vector", lambda e: e.memset(ones_br[:, :], 1.0))
    P.op("vector", lambda e: e.memset(ones_fr[:, :], 1.0))
    P.op("vector", lambda e: e.memset(ones_fc[:, :], 1.0), inc=sV.inc())
    v_ones = sV.at()

    qG.group()
    for dst, src in [(ident[:, :], d_ident[:, :]), (swapm[:, :], d_swap[:, :]),
                     (bqk[:, :], d_bqk[:, :]), (bvrow[:, :], d_bvrow[:, :]),
                     (borow[:, :], d_borow[:, :]),
                     (own0sb[:, :], d_own0[:, :])]:
        qG.dma(lambda e, dst=dst, src=src: e.dma_start(out=dst, in_=src))
    d_const = qG.done()

    a_cs = c_bf(0, T)
    a_sn = c_bf(8192, T)
    a_wqb = c_bf(16384, HN * 2048)
    a_wkb = c_bf(49152, HN * 2048)
    a_wvb = c_bf(81920, 16 * 1024)
    a_xt = [c_f32(114688 + 8192 * i, 2048) for i in range(2)]
    a_sq = c_f32(131072, 2048)
    a_xnT2 = [c_bf(139264 + 16384 * i, 16 * 512) for i in range(2)]
    a_xnt = [c_bf(172032 + 4096 * i, 2048) for i in range(2)]
    a_kbr = [c_bf(180224 + 2048 * i, 1024) for i in range(2)]
    a_rtp2 = c_f32(184320, 1024)
    a_kro = [c_bf(188416 + 2048 * i, 1024) for i in range(2)]
    a_vst = [c_bf(192512 + 2048 * i, 1024) for i in range(2)]

    qG.group()
    qG.dma(lambda e: e.dma_start(out=a_cs, in_=d_cs[:, :]))
    qG.dma(lambda e: e.dma_start(out=a_sn, in_=d_sn[:, :]))
    for h in range(HN):
        qG.dma(lambda e, h=h: e.dma_start(out=a_wqb[:, h * 2048:(h + 1) * 2048], in_=d_wq[h, :, :]))
        qG.dma(lambda e, h=h: e.dma_start(out=a_wkb[:, h * 2048:(h + 1) * 2048], in_=d_wk[h, :, :]))
    qG.dma(lambda e: e.dma_start(out=a_wvb, in_=d_wv[:, :]))
    d_wload = qG.done()

    xn_written = {}

    def emit_ln_tile(slot_idx, src_ap, dst_ap, tap_ap=None):
        buf = slot_idx % 2
        st = ln_slot(slot_idx)
        qS.group()
        qS.dma(lambda e, src_ap=src_ap, buf=buf: e.dma_start(out=a_xt[buf], in_=src_ap),
               waits=[free.get(("xt", buf))])
        m_x = qS.done()
        P.op("vector", lambda e, st=st, buf=buf: e.reduce_sum(st[:, 0:1], a_xt[buf], axis=AX.X),
             waits=[m_x], inc=sV.inc())
        P.op("vector", lambda e, st=st: e.tensor_scalar(out=st[:, 1:2], in0=st[:, 0:1],
                                                        scalar1=-1.0 / 2048.0, scalar2=None, op0=ALU.mult),
             waits=[sV.at()], inc=sV.inc())
        m_negm = sV.at()
        P.op("scalar", lambda e, st=st, buf=buf: e.activation(out=a_sq, in_=a_xt[buf], func=AF.Square,
                                                              bias=st[:, 1:2], scale=1.0,
                                                              accum_out=st[:, 2:3]),
             waits=[m_negm], inc=sA.inc())
        P.op("vector", lambda e, st=st: e.tensor_scalar(out=st[:, 3:4], in0=st[:, 2:3],
                                                        scalar1=1.0 / 2048.0, scalar2=EPS,
                                                        op0=ALU.mult, op1=ALU.add),
             waits=[sA.at()], inc=sV.inc())
        P.op("scalar", lambda e, st=st: e.activation(out=st[:, 4:5], in_=st[:, 3:4], func=AF.Sqrt),
             waits=[sV.at()], inc=sA.inc())
        P.op("vector", lambda e, st=st: e.reciprocal(st[:, 5:6], st[:, 4:5]),
             waits=[sA.at()], inc=sV.inc())
        P.op("vector", lambda e, st=st: e.tensor_scalar(out=st[:, 6:7], in0=st[:, 1:2],
                                                        scalar1=st[:, 5:6], scalar2=None, op0=ALU.mult),
             waits=[sV.at()], inc=sV.inc())
        m_rstd = sV.at()
        P.op("scalar", lambda e, st=st, buf=buf: e.activation(out=a_xnt[buf], in_=a_xt[buf],
                                                              func=AF.Identity, bias=st[:, 6:7],
                                                              scale=st[:, 5:6]),
             waits=[m_rstd, free.get(("xnt", buf))], inc=sA.inc())
        free[("xt", buf)] = sA.at()
        qS.group()
        qS.dma(lambda e, dst_ap=dst_ap, buf=buf: e.dma_start(out=dst_ap, in_=a_xnt[buf]),
               waits=[sA.at(), d_const])
        if tap_ap is not None:
            qS.dma(lambda e, tap_ap=tap_ap, buf=buf: e.dma_start(out=tap_ap, in_=a_xnt[buf]))
        mark = qS.done()
        free[("xnt", buf)] = mark
        return mark

    def emit_ln_pair(wp2):
        for s2 in range(8):
            ti = wp2 * 8 + s2
            xn_written[ti] = emit_ln_tile(
                ti, d_x[ti * 128:(ti + 1) * 128, :], s_xn[ti * 128:(ti + 1) * 128, :],
                taps["xn"][ti * 128:(ti + 1) * 128, :] if "xn" in taps else None)

    def emit_ln_own(wp2):
        for t2 in range(4 * wp2, 4 * wp2 + 4):
            emit_ln_tile(32 + t2, d_xo[t2 * 128:(t2 + 1) * 128, :], s_xnh[t2 * 128:(t2 + 1) * 128, :])

    emit_ln_pair(0)
    for wp in range(4):
        w0 = 2 * wp

        for half in range(2):
            w = w0 + half
            qA.group()
            for jc in range(16):
                qA.dma(lambda e, w=w, jc=jc, half=half: e.dma_start(
                       out=a_xnT2[half][:, jc * 512:(jc + 1) * 512],
                       in_=s_xn[w * 512:(w + 1) * 512, jc * 128:(jc + 1) * 128],
                       transpose=True),
                       waits=[[xn_written[w * 4 + s2] for s2 in range(4)],
                              free.get(("xnT", half))])
            if half == 0:
                m_xnT_a = qA.done()
            else:
                m_xnT_b = qA.done()

        kq_ps = {0: [p4, p5], 1: [p6, p7]}
        for h in range(HN):
            for which in range(2):
                wb = a_wkb if which == 0 else a_wqb
                pa_, pb_ = kq_ps[which]
                pname = ("kqp", which)
                scr = s_kT if which == 0 else s_qT
                tap = taps.get("kT" if which == 0 else "q")
                bias_col = bqk[:, 8 + h:9 + h] if which == 0 else bqk[:, h:h + 1]
                for jc in range(16):
                    P.op("tensor",
                         lambda e, wb=wb, pa_=pa_, h=h, jc=jc: e.matmul(
                             pa_[:, :], lhsT=wb[:, h * 2048 + jc * 128: h * 2048 + (jc + 1) * 128],
                             rhs=a_xnT2[0][:, jc * 512:(jc + 1) * 512], start=(jc == 0), stop=(jc == 15)),
                         waits=([m_xnT_a, d_wload, d_const, v_ones, free.get(pname)] if jc == 0 else []))
                    P.op("tensor",
                         lambda e, wb=wb, pb_=pb_, h=h, jc=jc: e.matmul(
                             pb_[:, :], lhsT=wb[:, h * 2048 + jc * 128: h * 2048 + (jc + 1) * 128],
                             rhs=a_xnT2[1][:, jc * 512:(jc + 1) * 512], start=(jc == 0), stop=(jc == 15)),
                         waits=([m_xnT_b] if jc == 0 else []),
                         inc=(sP.inc() if jc == 15 else None))
                m_proj = sP.at()
                kb = which
                P.op("scalar",
                     lambda e, pa_=pa_, bias_col=bias_col, kb=kb: e.activation(
                         out=a_kbr[kb][:, 0:512], in_=pa_[:, :], func=AF.Identity, bias=bias_col, scale=1.0),
                     waits=[m_proj, free.get(("kbr", kb))], inc=sA.inc())
                P.op("scalar",
                     lambda e, pb_=pb_, bias_col=bias_col, kb=kb: e.activation(
                         out=a_kbr[kb][:, 512:1024], in_=pb_[:, :], func=AF.Identity, bias=bias_col,
                         scale=1.0), inc=sA.inc())
                m_evac = sA.at()
                P.op("tensor",
                     lambda e, pa_=pa_, kb=kb: e.matmul(pa_[:, :], lhsT=swapm[:, :],
                                                        rhs=a_kbr[kb][:, 0:512], start=True, stop=True),
                     waits=[m_evac])
                P.op("tensor",
                     lambda e, pb_=pb_, kb=kb: e.matmul(pb_[:, :], lhsT=swapm[:, :],
                                                        rhs=a_kbr[kb][:, 512:1024], start=True, stop=True),
                     inc=sP.inc())
                m_swap = sP.at()
                cs = a_cs[:, w0 * 512: w0 * 512 + 1024]
                sn0 = a_sn[:, w0 * 512: w0 * 512 + 512]
                sn1 = a_sn[:, w0 * 512 + 512: w0 * 512 + 1024]
                kro = a_kro[kb]
                P.op("gpsimd", lambda e, kb=kb, cs=cs: e.tensor_tensor(out=a_rtp2, in0=a_kbr[kb],
                                                                       in1=cs, op=ALU.mult),
                     waits=[m_evac, free.get("rtp")], inc=sG.inc())
                m_t1 = sG.at()
                P.op("vector",
                     lambda e, pa_=pa_, sn0=sn0, kro=kro: e.scalar_tensor_tensor(
                         out=kro[:, 0:512], in0=pa_[:, :], scalar=1.0, in1=sn0, op0=ALU.mult,
                         op1=ALU.mult),
                     waits=[m_swap, free.get(("kro", kb))], inc=sV.inc())
                P.op("vector",
                     lambda e, pb_=pb_, sn1=sn1, kro=kro: e.scalar_tensor_tensor(
                         out=kro[:, 512:1024], in0=pb_[:, :], scalar=1.0, in1=sn1, op0=ALU.mult,
                         op1=ALU.mult), inc=sV.inc())
                P.op("vector", lambda e, kro=kro: e.tensor_tensor(out=kro, in0=kro, in1=a_rtp2,
                                                                  op=ALU.add),
                     waits=[m_t1, sV.at()], inc=sV.inc())
                m_rope = sV.at()
                free[("kbr", kb)] = m_t1
                free["rtp"] = m_rope
                free[pname] = sV.at()
                qS.group()
                qS.dma(lambda e, scr=scr, h=h, w0=w0, kro=kro: e.dma_start(
                       out=scr[h, :, w0 * 512: w0 * 512 + 1024], in_=kro),
                       waits=[m_rope])
                if tap is not None:
                    qS.dma(lambda e, tap=tap, h=h, w0=w0, kro=kro: e.dma_start(
                           out=tap[h, :, w0 * 512: w0 * 512 + 1024], in_=kro))
                free[("kro", kb)] = qS.done()

        if wp + 1 < 4:
            emit_ln_pair(wp + 1)
        emit_ln_own(wp)

        v_ps = [pA[:, 0:512], pA[:, 512:1024], pB[:, 0:512], pB[:, 512:1024]]
        for s in range(8):
            half = s // 4
            m_xnT_h = m_xnT_a if half == 0 else m_xnT_b
            b0 = 2 * (s % 2)
            pv0, pv1 = v_ps[b0], v_ps[b0 + 1]
            for jc in range(16):
                P.op("tensor",
                     lambda e, pv0=pv0, s=s, jc=jc, half=half: e.matmul(
                         pv0, lhsT=a_xnT2[half][:, jc * 512 + (s % 4) * 128: jc * 512 + (s % 4) * 128 + 128],
                         rhs=a_wvb[:, jc * 1024: jc * 1024 + 512],
                         start=(jc == 0), stop=False),
                     waits=([m_xnT_h, free.get(("vps", b0))] if jc == 0 else []))
                P.op("tensor",
                     lambda e, pv1=pv1, s=s, jc=jc, half=half: e.matmul(
                         pv1, lhsT=a_xnT2[half][:, jc * 512 + (s % 4) * 128: jc * 512 + (s % 4) * 128 + 128],
                         rhs=a_wvb[:, jc * 1024 + 512: jc * 1024 + 1024],
                         start=(jc == 0), stop=False),
                     waits=([free.get(("vps", b0 + 1))] if jc == 0 else []))
            P.op("tensor",
                 lambda e, pv0=pv0: e.matmul(pv0, lhsT=ones_br[0:1, :], rhs=bvrow[0:1, 0:512],
                                             start=False, stop=True))
            P.op("tensor",
                 lambda e, pv1=pv1: e.matmul(pv1, lhsT=ones_br[0:1, :], rhs=bvrow[0:1, 512:1024],
                                             start=False, stop=True),
                 inc=sP.inc())
            m_vp = sP.at()
            P.op("scalar",
                 lambda e, pv0=pv0, s=s: e.activation(out=a_vst[s % 2][:, 0:512], in_=pv0,
                                                      func=AF.Identity, scale=1.0),
                 waits=[m_vp, free.get(("vst", s % 2))], inc=sA.inc())
            free[("vps", b0)] = sA.at()
            P.op("scalar",
                 lambda e, pv1=pv1, s=s: e.activation(out=a_vst[s % 2][:, 512:1024], in_=pv1,
                                                      func=AF.Identity, scale=1.0), inc=sA.inc())
            free[("vps", b0 + 1)] = sA.at()
            m_vst = sA.at()
            qS.group()
            qS.dma(lambda e, w0=w0, s=s: e.dma_start(
                   out=s_v[w0 * 512 + s * 128: w0 * 512 + (s + 1) * 128, :], in_=a_vst[s % 2]),
                   waits=[m_vst])
            if "v" in taps:
                qS.dma(lambda e, w0=w0, s=s: e.dma_start(
                       out=taps["v"][w0 * 512 + s * 128: w0 * 512 + (s + 1) * 128, :], in_=a_vst[s % 2]))
            free[("vst", s % 2)] = qS.done()
            if s == 3:
                free[("xnT", 0)] = sP.at()
        free[("xnT", 1)] = sP.at()
    xnh_written = qS.all_done()

    mB_pe = sP.at()
    mB_act = sA.at()
    mB_dve = sV.at()
    mB_pool = sG.at()
    mB_sync = qS.all_done()

    a_vb = c_bf(0, 32 * 1024)
    a_zb = c_bf(65536, HN * T)
    KQ0 = 131072
    a_kq = [(c_bf(KQ0 + 16384 * i, T), c_bf(KQ0 + 16384 * i + 8192, T)) for i in range(2)]
    a_pT = c_bf(163840, 4 * 512)
    a_ktmp = c_bf(172032, T)
    a_qtmp = c_bf(180224, T)
    a_recipb = c_f32(190464, 512)
    a_masks = [c_bf(188416, 1024), c_bf(192512, 1024)]

    mB_all = [mB_pe, mB_act, mB_dve, mB_pool, mB_sync, qA.all_done()]
    qS.group()
    qS.dma(lambda e: e.dma_start(out=a_vb, in_=s_v.rearrange("(w p) d -> p w d", p=128)),
           waits=mB_all)
    m_vb = qS.done()
    qG.group()
    qG.dma(lambda e: e.dma_start(out=a_masks[0], in_=d_mask[:, 0:1024]), waits=mB_all)
    qG.dma(lambda e: e.dma_start(out=a_masks[1], in_=d_mask[:, 1024:2048]))
    d_masks = qG.done()
    kq_loaded = {}
    for h in range(2):
        qS.group()
        qS.dma(lambda e, h=h: e.dma_start(out=a_kq[h % 2][0], in_=s_kT[h, :, :]), waits=mB_all)
        qS.dma(lambda e, h=h: e.dma_start(out=a_kq[h % 2][1], in_=s_qT[h, :, :]))
        kq_loaded[h] = qS.done()

    a_wdc = [c_bf(163840 + 4096 + 2048 * i, 1024) for i in range(2)]
    for ic in range(64):
        for hf in range(2):
            j = (ic * 2 + hf) % 2
            qG.group()
            qG.dma(lambda e, ic=ic, hf=hf, j=j: e.dma_start(out=a_wdc[j],
                                                            in_=d_wd[ic, :, hf * 1024:(hf + 1) * 1024]),
                   waits=[mB_pool if ic == 0 else None, mB_pe if ic == 0 else None,
                          free.get(("wdc", j))])
            m_ld = qG.done()
            qG.group()
            qG.dma(lambda e, ic=ic, hf=hf, j=j: e.dma_start(out=s_wd[ic, :, hf * 1024:(hf + 1) * 1024],
                                                            in_=a_wdc[j]),
                   waits=[m_ld])
            free[("wdc", j)] = qG.done()
    d_wdcast = qG.all_done()

    m_lastexp = {}
    pending_norm = []

    def flush_norm():
        if not pending_norm:
            return
        hh_, gg_, pz_, m_mm = pending_norm.pop()
        P.op("vector", lambda e: e.tensor_scalar(out=mc_sums, in0=p4[0:1, :], scalar1=1e-30, scalar2=None,
                                                 op0=ALU.max),
             waits=[m_mm], inc=sV.inc())
        P.op("vector", lambda e: e.reciprocal(mc_recip, mc_sums), waits=[sV.at()], inc=sV.inc())
        P.op("tensor", lambda e: e.matmul(p4[:, 0:512], lhsT=ones_fr[0:1, :], rhs=mc_recip,
                                          start=True, stop=True),
             waits=[sV.at()], inc=sP.inc())
        P.op("vector", lambda e: e.tensor_copy(out=a_recipb, in_=p4[:, 0:512]),
             waits=[sP.at()], inc=sV.inc())
        free["p4sum"] = sV.at()
        P.op("vector",
             lambda e, hh_=hh_, gg_=gg_, pz_=pz_: e.tensor_tensor(
                 out=a_zb[:, hh_ * T + gg_ * 512: hh_ * T + (gg_ + 1) * 512],
                 in0=pz_[:, :], in1=a_recipb, op=ALU.mult),
             waits=[sV.at()], inc=sV.inc())
        free[("pz", gg_ % 2)] = sV.at()

    for h in range(HN):
        K_h, Q_h = a_kq[h % 2]
        mKQ = kq_loaded[h]
        mc_ksum = mc_ksum_s[h % 2]
        mc_kbarsq = mc_kbarsq_s[h % 2]
        mc_negB = mc_negB_s[h % 2]
        kbarb = kbarb_s[h % 2]
        P.op("scalar", lambda e, K_h=K_h: e.activation(out=a_ktmp, in_=K_h, func=AF.Square),
             waits=[mKQ, mB_all[1] if h == 0 else None, free.get("ktmp")], inc=sA.inc())
        m_ksq = sA.at()
        P.op("scalar", lambda e, Q_h=Q_h: e.activation(out=a_qtmp, in_=Q_h, func=AF.Square),
             waits=[free.get("qtmp")], inc=sA.inc())
        m_qsq = sA.at()
        P.op("vector", lambda e, K_h=K_h, mc_ksum=mc_ksum: e.reduce_sum(mc_ksum, K_h, axis=AX.X),
             waits=[mKQ, m_lastexp.get(h - 2)], inc=sV.inc())
        P.op("vector", lambda e, kbarb=kbarb, mc_ksum=mc_ksum: e.tensor_copy(out=kbarb[:, :], in_=mc_ksum),
             waits=[sV.at()], inc=sV.inc())
        m_kbar = sV.at()
        P.op("scalar", lambda e, mc_kbarsq=mc_kbarsq, mc_ksum=mc_ksum: e.activation(
             out=mc_kbarsq, in_=mc_ksum, func=AF.Square),
             waits=[m_kbar], inc=sA.inc())
        m_kbarsq = sA.at()
        for kc in range(32):
            P.op("tensor", lambda e, kc=kc: e.matmul(p7[:, kc:kc + 1],
                                                     lhsT=a_ktmp[:, kc * 128:(kc + 1) * 128], rhs=ones_bc[:, :],
                                                     start=True, stop=True),
                 waits=([m_ksq, free.get(("kqp", 1, 1)), free.get("p7")] if kc == 0 else []))
        for kc in range(32):
            P.op("tensor", lambda e, kc=kc, K_h=K_h, kbarb=kbarb: e.matmul(
                 p7[:, 32 + kc:33 + kc], lhsT=K_h[:, kc * 128:(kc + 1) * 128],
                 rhs=kbarb[:, :], start=True, stop=True),
                 waits=([m_kbar] if kc == 0 else []))
        for kc in range(32):
            P.op("tensor", lambda e, kc=kc: e.matmul(p7[:, 64 + kc:65 + kc],
                                                     lhsT=a_qtmp[:, kc * 128:(kc + 1) * 128], rhs=ones_bc[:, :],
                                                     start=True, stop=True),
                 waits=([m_qsq] if kc == 0 else []))
        for kc in range(32):
            P.op("tensor", lambda e, kc=kc, Q_h=Q_h, kbarb=kbarb: e.matmul(
                 p7[:, 96 + kc:97 + kc], lhsT=Q_h[:, kc * 128:(kc + 1) * 128],
                 rhs=kbarb[:, :], start=True, stop=True))
        P.op("tensor", lambda e, mc_kbarsq=mc_kbarsq: e.matmul(p7[0:1, 128:129], lhsT=mc_kbarsq,
                                                               rhs=ones_fc[:, :], start=True, stop=True),
             waits=[m_kbarsq], inc=sP.inc())
        m_stats = sP.at()
        free["ktmp"] = m_stats
        free["qtmp"] = m_stats
        P.op("vector", lambda e: e.tensor_copy(out=mc_nk, in_=p7[:, 0:32]), waits=[m_stats], inc=sV.inc())
        P.op("vector", lambda e: e.tensor_copy(out=mc_dk, in_=p7[:, 32:64]), inc=sV.inc())
        P.op("vector", lambda e: e.tensor_copy(out=mc_nq, in_=p7[:, 64:96]), inc=sV.inc())
        P.op("vector", lambda e: e.tensor_copy(out=mc_dq, in_=p7[:, 96:128]), inc=sV.inc())
        P.op("vector", lambda e: e.tensor_scalar(out=mc_t11a, in0=p7[0:1, 128:129],
                                                 scalar1=1.0 / (4096.0 * 4096.0), scalar2=None, op0=ALU.mult),
             inc=sV.inc())
        m_ev = sV.at()
        P.op("vector", lambda e: e.scalar_tensor_tensor(out=mc_r2, in0=mc_dk, scalar=-2.0 / 4096.0, in1=mc_nk,
                                                        op0=ALU.mult, op1=ALU.add),
             waits=[m_ev], inc=sV.inc())
        m_r2a = sV.at()
        P.op("tensor", lambda e: e.matmul(p7[:, 136:137], lhsT=ones_fr[0:1, :], rhs=mc_t11a,
                                          start=True, stop=True),
             waits=[m_ev], inc=sP.inc())
        P.op("vector", lambda e: e.tensor_copy(out=mc_ksb, in_=p7[:, 136:137]), waits=[sP.at()], inc=sV.inc())
        P.op("vector", lambda e: e.tensor_scalar(out=mc_r2, in0=mc_r2, scalar1=mc_ksb, scalar2=None, op0=ALU.add),
             waits=[sV.at(), m_r2a], inc=sV.inc())
        P.op("vector", lambda e: e.reduce_max(mc_rmax, mc_r2, axis=AX.X), waits=[sV.at()], inc=sV.inc())
        P.op("tensor", lambda e: e.transpose(p7[0:1, 144:272], mc_rmax, ident[:, :]),
             waits=[sV.at()], inc=sP.inc())
        P.op("vector", lambda e: e.reduce_max(mc_t11b, p7[0:1, 144:272], axis=AX.X),
             waits=[sP.at()], inc=sV.inc())
        P.op("scalar", lambda e: e.activation(out=mc_t11a, in_=mc_t11b, func=AF.Sqrt),
             waits=[sV.at()], inc=sA.inc())
        P.op("tensor", lambda e: e.matmul(p7[:, 136:137], lhsT=ones_fr[0:1, :], rhs=mc_t11a,
                                          start=True, stop=True),
             waits=[sA.at()], inc=sP.inc())
        P.op("vector", lambda e: e.tensor_scalar(out=mc_mr, in0=p7[:, 136:137], scalar1=GAMMA, scalar2=None,
                                                 op0=ALU.mult),
             waits=[sP.at()], inc=sV.inc())
        m_mr = sV.at()
        P.op("scalar", lambda e: e.activation(out=mc_qn, in_=mc_nq, func=AF.Sqrt),
             waits=[m_ev], inc=sA.inc())
        m_qn = sA.at()
        P.op("vector", lambda e: e.tensor_scalar(out=mc_dq, in0=mc_dq, scalar1=GAMMA / 4096.0, scalar2=None,
                                                 op0=ALU.mult),
             waits=[m_ev], inc=sV.inc())
        P.op("vector", lambda e: e.scalar_tensor_tensor(out=mc_bq, in0=mc_qn, scalar=mc_mr, in1=mc_dq,
                                                        op0=ALU.mult, op1=ALU.add),
             waits=[m_mr, m_qn, sV.at()], inc=sV.inc())
        P.op("vector", lambda e: e.reduce_max(mc_rmax, mc_bq, axis=AX.X), waits=[sV.at()], inc=sV.inc())
        P.op("tensor", lambda e: e.transpose(p7[0:1, 144:272], mc_rmax, ident[:, :]),
             waits=[sV.at()], inc=sP.inc())
        P.op("vector", lambda e: e.reduce_max(mc_t11b, p7[0:1, 144:272], axis=AX.X),
             waits=[sP.at()], inc=sV.inc())
        P.op("vector", lambda e: e.tensor_scalar(out=mc_t11a, in0=mc_t11b, scalar1=-1.0, scalar2=-1.0,
                                                 op0=ALU.mult, op1=ALU.add),
             waits=[sV.at()], inc=sV.inc())
        P.op("tensor", lambda e: e.matmul(p7[:, 136:137], lhsT=ones_fr[0:1, :], rhs=mc_t11a,
                                          start=True, stop=True),
             waits=[sV.at()], inc=sP.inc())
        P.op("vector", lambda e, mc_negB=mc_negB: e.tensor_copy(out=mc_negB, in_=p7[:, 136:137]),
             waits=[sP.at()], inc=sV.inc())
        m_negB = sV.at()
        free["p7"] = m_negB

        if 1 <= h < HN - 1:
            qS.group()
            qS.dma(lambda e, h=h: e.dma_start(out=a_kq[(h + 1) % 2][0], in_=s_kT[h + 1, :, :]),
                   waits=[free.get(("kqbuf", (h + 1) % 2))])
            qS.dma(lambda e, h=h: e.dma_start(out=a_kq[(h + 1) % 2][1], in_=s_qT[h + 1, :, :]))
            kq_loaded[h + 1] = qS.done()

        for g in range(8):
            KC = 4 * (g + 1)
            pairs = KC // 2
            pz = [p5, p6][g % 2]
            m_group_mm = None
            for pr in range(pairs):
                kc0 = 2 * pr
                psX = [pA, pB][pr % 2]
                P.op("tensor",
                     lambda e, psX=psX, K_h=K_h, Q_h=Q_h, kc0=kc0, g=g: e.matmul(
                         psX[:, 0:512], lhsT=K_h[:, kc0 * 128:(kc0 + 1) * 128],
                         rhs=Q_h[:, g * 512:(g + 1) * 512], start=True, stop=True),
                     waits=[m_vb if (h == 0 and g == 0 and pr == 0) else None,
                            free.get(("spair", pr % 2)), m_negB if pr == 0 else None])
                P.op("tensor",
                     lambda e, psX=psX, K_h=K_h, Q_h=Q_h, kc0=kc0, g=g: e.matmul(
                         psX[:, 512:1024], lhsT=K_h[:, (kc0 + 1) * 128:(kc0 + 2) * 128],
                         rhs=Q_h[:, g * 512:(g + 1) * 512], start=True, stop=True),
                     inc=sP.inc())
                m_spair = sP.at()
                if pr == 0:
                    flush_norm()
                slot = (kc0 % 4) * 512
                P.op("scalar",
                     lambda e, psX=psX, slot=slot, mc_negB=mc_negB: e.activation(
                         out=a_pT[:, slot:slot + 1024], in_=psX[:, :],
                         func=AF.Exp, bias=mc_negB, scale=GAMMA),
                     waits=[m_spair], inc=sA.inc())
                m_exp = sA.at()
                free[("spair", pr % 2)] = m_exp
                m_pr = m_exp
                if pr >= pairs - 2:
                    mi = pr - (pairs - 2)
                    P.op("vector",
                         lambda e, slot=slot, mi=mi: e.tensor_tensor(out=a_pT[:, slot:slot + 1024],
                                                                     in0=a_pT[:, slot:slot + 1024],
                                                                     in1=a_masks[mi], op=ALU.mult),
                         waits=[m_exp, d_masks], inc=sV.inc())
                    m_pr = sV.at()
                for q in range(2):
                    kc = kc0 + q
                    sl = ((kc % 4)) * 512
                    P.op("tensor",
                         lambda e, kc=kc, sl=sl, KC=KC: e.matmul(p4[0:1, :], lhsT=ones_bc[:, :],
                                                                 rhs=a_pT[:, sl:sl + 512],
                                                                 start=(kc == 0), stop=(kc == KC - 1)),
                         waits=[m_pr, free.get("p4sum") if kc == 0 else None])
                    P.op("tensor",
                         lambda e, kc=kc, sl=sl, h=h, pz=pz, KC=KC: e.matmul(
                             pz[:, :], lhsT=a_vb[:, kc * 1024 + h * 128: kc * 1024 + (h + 1) * 128],
                             rhs=a_pT[:, sl:sl + 512], start=(kc == 0), stop=(kc == KC - 1)),
                         waits=[free.get(("pz", g % 2)) if kc == 0 else None],
                         inc=(sP.inc() if kc == KC - 1 else None))
            m_group_mm = sP.at()
            pending_norm.append((h, g, pz, m_group_mm))
        flush_norm()
        m_lastexp[h] = sA.at()
        free[("kqbuf", h % 2)] = sP.at()

    mD_pe = sP.at()
    mD_dve = sV.at()
    if "z" in taps:
        qS.group()
        qS.dma(lambda e: e.dma_start(out=taps["z"][:, :], in_=a_zb), waits=[mD_dve])

    a_wob = c_bf(KQ0, HN * 2048)
    a_est = [c_bf(163840 + 1024 * i, 512) for i in range(8)]

    qG.group()
    qG.dma(lambda e: e.dma_start(out=a_wob, in_=d_wo[:, :]), waits=[mD_pe, d_wdcast])
    m_wob = qG.done()

    a_xno = c_bf(0, 16 * 2048)
    qA.group()
    for jc in range(16):
        for tq in range(4):
            qA.dma(lambda e, jc=jc, tq=tq: e.dma_start(
                   out=a_xno[:, jc * 2048 + tq * 512: jc * 2048 + (tq + 1) * 512],
                   in_=s_xnh[tq * 512:(tq + 1) * 512, jc * 128:(jc + 1) * 128],
                   transpose=True),
                   waits=([mD_pe, mD_dve] + [xnh_written] if (jc == 0 and tq == 0) else []))
    m_xno_early = qA.done()

    e_ps2 = [[pA[:, 0:512], pA[:, 512:1024], pB[:, 0:512], pB[:, 512:1024]],
             [p4[:, :], p5[:, :], p6[:, :], p7[:, :]]]
    for qt in range(32):
        par = qt % 2
        banks = e_ps2[par]
        for mg in range(4):
            for hh in range(HN):
                P.op("tensor",
                     lambda e, banks=banks, qt=qt, mg=mg, hh=hh: e.matmul(
                         banks[mg], lhsT=a_zb[:, hh * T + qt * 128: hh * T + (qt + 1) * 128],
                         rhs=a_wob[:, hh * 2048 + mg * 512: hh * 2048 + (mg + 1) * 512],
                         start=(hh == 0), stop=False),
                     waits=([m_wob, mD_dve, mD_pe, free.get(("eps", par, mg))]
                            if hh == 0 else []))
        for mg in range(4):
            P.op("tensor",
                 lambda e, banks=banks, mg=mg: e.matmul(banks[mg], lhsT=ones_br[0:1, :],
                                                        rhs=borow[0:1, mg * 512:(mg + 1) * 512],
                                                        start=False, stop=True),
                 inc=(sP.inc() if mg == 3 else None))
        m_e = sP.at()
        qS.group()
        for mg in range(4):
            P.op("scalar", lambda e, banks=banks, mg=mg, par=par: e.activation(
                 out=a_est[4 * par + mg], in_=banks[mg], func=AF.Identity, scale=1.0),
                 waits=[m_e, free.get(("est", 4 * par + mg))], inc=sA.inc())
            m_ee = sA.at()
            free[("eps", par, mg)] = m_ee
            qS.dma(lambda e, qt=qt, mg=mg, par=par: e.dma_start(
                   out=s_ap[qt * 128:(qt + 1) * 128, mg * 512:(mg + 1) * 512],
                   in_=a_est[4 * par + mg]),
                   waits=[m_ee])
            if "ap" in taps:
                qS.dma(lambda e, qt=qt, mg=mg, par=par: e.dma_start(
                       out=taps["ap"][qt * 128:(qt + 1) * 128, mg * 512:(mg + 1) * 512],
                       in_=a_est[4 * par + mg]))
        m_apg = qS.done()
        for mg in range(4):
            free[("est", 4 * par + mg)] = m_apg
    m_ap = qS.all_done()

    P.op("gpsimd", lambda e: e.collective_compute(
        "ReduceScatter", ALU.add, replica_groups=[[0, 1], [2, 3], [4, 5], [6, 7]],
        ins=[s_ap.ap().opt()], outs=[s_ar.ap().opt()]),
        waits=[m_ap], inc=sC.inc())

    a_wgb = [c_bf(65536 + 4096 * i, 2048) for i in range(4)]
    a_wub = [c_bf(81920 + 4096 * i, 2048) for i in range(4)]
    a_wgb2 = [c_bf(98304 + 4096 * i, 2048) for i in range(4)]
    a_wub2 = [c_bf(114688 + 4096 * i, 2048) for i in range(4)]
    a_h = [c_f32(131072 + 2048 * i, 512) for i in range(16)]
    a_habs = [c_f32(163840 + 2048 * i, 512) for i in range(4)]
    a_tau = c_f32(172032, 512)
    a_msk = c_f32(174080, 512)
    a_gs = c_f32(176128, 512)
    a_hst = [c_bf(178176 + 1024 * i, 512) for i in range(4)]

    mE_done = [sP.at(), sA.at(), qS.all_done()]
    m_xno = m_xno_early

    wgsets = [(a_wgb, a_wub), (a_wgb2, a_wub2)]
    wg_loaded = {}

    def emit_wload(gc):
        s = gc % 2
        gset, uset = wgsets[s]
        qG.group()
        for j in range(4):
            it = j * 16 + gc
            qG.dma(lambda e, it=it, j=j, gset=gset: e.dma_start(out=gset[j], in_=d_wg[it, :, :]),
                   waits=[free.get(("wgs", s)), mE_done[0] if gc < 2 else None])
            qG.dma(lambda e, it=it, j=j, uset=uset: e.dma_start(out=uset[j], in_=d_wu[it, :, :]))
        wg_loaded[gc] = qG.done()

    emit_wload(0)
    emit_wload(1)

    g_ps = [pA[:, 0:512], pA[:, 512:1024], pB[:, 0:512], pB[:, 512:1024]]
    d_ps = [p4[:, :], p5[:, :], p6[:, :], p7[:, :]]
    for gc in range(16):
        s = gc % 2
        gset, uset = wgsets[s]
        for j in range(4):
            for tq in range(4):
                for jc in range(16):
                    P.op("tensor",
                         lambda e, gset=gset, j=j, jc=jc, tq=tq: e.matmul(
                             g_ps[tq], lhsT=gset[j][:, jc * 128:(jc + 1) * 128],
                             rhs=a_xno[:, jc * 2048 + tq * 512: jc * 2048 + (tq + 1) * 512],
                             start=(jc == 0), stop=(jc == 15)),
                         waits=([wg_loaded[gc], m_xno, free.get(("gps", tq))]
                                if (jc == 0 and tq == 0) else
                                ([free.get(("gps", tq))] if jc == 0 else [])),
                         inc=(sP.inc() if (jc == 15 and tq == 3) else None))
            m_g = sP.at()
            for tq in range(4):
                for jc in range(16):
                    P.op("tensor",
                         lambda e, uset=uset, j=j, jc=jc, tq=tq: e.matmul(
                             d_ps[tq], lhsT=uset[j][:, jc * 128:(jc + 1) * 128],
                             rhs=a_xno[:, jc * 2048 + tq * 512: jc * 2048 + (tq + 1) * 512],
                             start=(jc == 0), stop=(jc == 15)),
                         waits=([free.get(("dps", tq))] if jc == 0 else []),
                         inc=(sP.inc() if (jc == 15 and tq == 3) else None))
            m_u = sP.at()
            for tq in range(4):
                P.op("scalar", lambda e, tq=tq: e.activation(out=a_gs, in_=g_ps[tq], func=AF.Silu),
                     waits=[m_g, free.get("gs")], inc=sA.inc())
                m_silu = sA.at()
                free[("gps", tq)] = m_silu
                P.op("vector", lambda e, j=j, tq=tq: e.tensor_tensor(out=a_h[4 * j + tq], in0=d_ps[tq],
                                                                     in1=a_gs, op=ALU.mult),
                     waits=[m_u, m_silu, free.get(("hbuf", 4 * j + tq))], inc=sV.inc())
                free[("dps", tq)] = sV.at()
                free["gs"] = sV.at()
        for tq in range(4):
            for j in range(4):
                P.op("scalar", lambda e, j=j, tq=tq: e.activation(out=a_habs[j], in_=a_h[4 * j + tq],
                                                                  func=AF.Abs),
                     waits=[sV.at() if j == 0 and tq == 0 else None, free.get(("habs", j))], inc=sA.inc())
            m_habs = sA.at()
            P.op("vector", lambda e: e.tensor_tensor(out=a_tau, in0=a_habs[0], in1=a_habs[1], op=ALU.max),
                 waits=[m_habs, free.get("tau")], inc=sV.inc())
            P.op("vector", lambda e: e.tensor_tensor(out=a_msk, in0=a_habs[2], in1=a_habs[3], op=ALU.max),
                 waits=[free.get("msk")], inc=sV.inc())
            P.op("vector", lambda e: e.tensor_tensor(out=a_tau, in0=a_tau, in1=a_msk, op=ALU.min),
                 waits=[sV.at()], inc=sV.inc())
            P.op("vector", lambda e: e.tensor_tensor(out=a_gs, in0=a_habs[0], in1=a_habs[1], op=ALU.min),
                 inc=sV.inc())
            P.op("vector", lambda e: e.tensor_tensor(out=a_msk, in0=a_habs[2], in1=a_habs[3], op=ALU.min),
                 waits=[sV.at()], inc=sV.inc())
            P.op("vector", lambda e: e.tensor_tensor(out=a_gs, in0=a_gs, in1=a_msk, op=ALU.max),
                 waits=[sV.at()], inc=sV.inc())
            P.op("vector", lambda e: e.tensor_tensor(out=a_tau, in0=a_tau, in1=a_gs, op=ALU.max),
                 waits=[sV.at()], inc=sV.inc())
            m_tau = sV.at()
            free["msk"] = m_tau
            free["gs"] = m_tau
            for j in range(4):
                P.op("vector", lambda e, j=j: e.tensor_tensor(out=a_msk, in0=a_habs[j], in1=a_tau, op=ALU.is_ge),
                     waits=[m_tau, sV.at()], inc=sV.inc())
                free[("habs", j)] = sV.at()
                P.op("vector", lambda e, j=j, tq=tq: e.tensor_tensor(out=a_hst[j], in0=a_h[4 * j + tq],
                                                                      in1=a_msk, op=ALU.mult),
                     waits=[sV.at(), free.get(("hst", j))], inc=sV.inc())
                free[("hbuf", 4 * j + tq)] = sV.at()
                free["msk"] = sV.at()
                free["tau"] = sV.at()
                m_hs = sV.at()
                it = j * 16 + gc
                qS.group()
                qS.dma(lambda e, it=it, tq=tq, j=j: e.dma_start(
                       out=s_hsp[it * 128:(it + 1) * 128, tq * 512:(tq + 1) * 512], in_=a_hst[j]),
                       waits=[m_hs])
                if "hsp" in taps:
                    qS.dma(lambda e, it=it, tq=tq, j=j: e.dma_start(
                           out=taps["hsp"][it * 128:(it + 1) * 128, tq * 512:(tq + 1) * 512], in_=a_hst[j]))
                free[("hst", j)] = qS.done()
        free[("wgs", s)] = sP.at()
        if gc + 2 < 16:
            emit_wload(gc + 2)
    m_gate_pe = sP.at()
    m_gate_dve = sV.at()
    m_gate_act = sA.at()
    m_gate_sync = qS.all_done()

    a_hw = c_bf(0, 64 * 1024)
    a_wdt = [c_bf(131072 + 1024 * i, 512) for i in range(4)]
    a_fst = [c_bf(135168 + 1024 * i, 512) for i in range(8)]

    d_banks = [pA[:, 0:512], pA[:, 512:1024], pB[:, 0:512], pB[:, 512:1024],
               p4[:, :], p5[:, :], p6[:, :], p7[:, :]]

    a_gx = [c_f32(143360 + 8192 * i, 2048) for i in range(2)]
    a_ga = [c_bf(159744 + 4096 * i, 2048) for i in range(2)]
    a_gf = [c_bf(167936 + 4096 * i, 2048) for i in range(2)]
    a_go = [c_f32(176128 + 8192 * i, 2048) for i in range(2)]

    def emit_G_half(gh, th_done):
        for tt in range(8 * gh, 8 * gh + 8):
            b = tt % 2
            qS.group()
            qS.dma(lambda e, tt=tt, b=b: e.dma_start(out=a_gx[b], in_=d_xo[tt * 128:(tt + 1) * 128, :]),
                   waits=(th_done if tt % 8 < 2 else []) + [free.get(("gx", b))])
            m_gx = qS.done()
            qS.group()
            qS.dma(lambda e, tt=tt, b=b: e.dma_start(out=a_ga[b], in_=s_ar[tt * 128:(tt + 1) * 128, :]),
                   waits=[(sC.h, 1) if tt == 8 * gh else None, free.get(("ga", b))])
            m_ga = qS.done()
            qS.group()
            qS.dma(lambda e, tt=tt, b=b: e.dma_start(out=a_gf[b], in_=s_ffn[tt * 128:(tt + 1) * 128, :]),
                   waits=[free.get(("gf", b))])
            m_gf = qS.done()
            P.op("vector", lambda e, b=b: e.tensor_tensor(out=a_go[b], in0=a_ga[b], in1=a_gf[b], op=ALU.add),
                 waits=[m_ga, m_gf, free.get(("go", b))], inc=sV.inc())
            free[("ga", b)] = sV.at()
            free[("gf", b)] = sV.at()
            P.op("vector", lambda e, b=b: e.tensor_tensor(out=a_go[b], in0=a_go[b], in1=a_gx[b], op=ALU.add),
                 waits=[m_gx, sV.at()], inc=sV.inc())
            free[("gx", b)] = sV.at()
            m_go = sV.at()
            qS.group()
            qS.dma(lambda e, tt=tt, b=b: e.dma_start(out=d_out[tt * 128:(tt + 1) * 128, :], in_=a_go[b]),
                   waits=[m_go])
            free[("go", b)] = qS.done()

    for th in range(2):
        qS.group()
        qS.dma(lambda e, th=th: e.dma_start(
               out=a_hw, in_=s_hsp[:, th * 1024:(th + 1) * 1024].rearrange("(ic p) t -> p ic t", p=128)),
               waits=[m_gate_pe, m_gate_sync, free.get("hwin")])
        m_hw = qS.done()
        for mg in range(4):
            for ic in range(64):
                wsl = ic % 4
                qA.group()
                qA.dma(lambda e, ic=ic, mg=mg, wsl=wsl: e.dma_start(out=a_wdt[wsl],
                                                                    in_=s_wd[ic, :, mg * 512:(mg + 1) * 512]),
                       waits=[d_wdcast if (th == 0 and mg == 0 and ic < 4) else None,
                              m_gate_pe if (th == 0 and mg == 0 and ic == 0) else None,
                              m_gate_dve if (th == 0 and mg == 0 and ic == 0) else None,
                              m_gate_act if (th == 0 and mg == 0 and ic == 0) else None,
                              free.get(("wdt", wsl))])
                m_wd = qA.done()
                for tt in range(8):
                    w_list = [m_wd]
                    if ic == 0:
                        w_list += [m_hw, free.get(("dbk", tt)),
                                   free.get(("gps", tt)) if tt < 4 else free.get(("dps", tt - 4))]
                    P.op("tensor",
                         lambda e, ic=ic, tt=tt, wsl=wsl: e.matmul(
                             d_banks[tt], lhsT=a_hw[:, ic * 1024 + tt * 128: ic * 1024 + (tt + 1) * 128],
                             rhs=a_wdt[wsl], start=(ic == 0), stop=(ic == 63)),
                         waits=w_list,
                         inc=(sP.inc() if tt == 7 else None))
                free[("wdt", wsl)] = sP.at()
            m_dmm = sP.at()
            for tt in range(8):
                P.op("scalar", lambda e, tt=tt: e.activation(out=a_fst[tt], in_=d_banks[tt],
                                                             func=AF.Identity, scale=1.0),
                     waits=[m_dmm, free.get(("fst", tt))], inc=sA.inc())
                free[("dbk", tt)] = sA.at()
                qS.group()
                qS.dma(lambda e, th=th, mg=mg, tt=tt: e.dma_start(
                       out=s_ffn[th * 1024 + tt * 128: th * 1024 + (tt + 1) * 128, mg * 512:(mg + 1) * 512],
                       in_=a_fst[tt]),
                       waits=[sA.at()])
                if "ffn" in taps:
                    qS.dma(lambda e, th=th, mg=mg, tt=tt: e.dma_start(
                           out=taps["ffn"][th * 1024 + tt * 128: th * 1024 + (tt + 1) * 128,
                                           mg * 512:(mg + 1) * 512], in_=a_fst[tt]))
                free[("fst", tt)] = qS.done()
        free["hwin"] = sP.at()
        if th == 0:
            th0_done = [sP.at(), sA.at(), qS.all_done()]
            emit_G_half(0, th0_done)
    m_down_pe = sP.at()
    m_down_sync = qS.all_done()
    emit_G_half(1, [m_down_pe, sA.at(), m_down_sync])


    P.op("sync", lambda e: e.nop(), waits=[qS.all_done(), qA.all_done(), qG.all_done()])

    with es, nc.Block() as block:
        block.sync(lambda e: P.replay("sync", e))
        block.scalar(lambda e: P.replay("scalar", e))
        block.vector(lambda e: P.replay("vector", e))
        block.tensor(lambda e: P.replay("tensor", e))
        block.gpsimd(lambda e: P.replay("gpsimd", e))
    return nc


def _rope_perm():
    p = np.empty(128, np.int64)
    p[0:64] = np.arange(64) * 2
    p[64:128] = np.arange(64) * 2 + 1
    return p


def _ffn_perm():
    ip = np.arange(II)
    return 4 * (ip % 2048) + (ip // 2048)


def _prep_inputs(inputs):
    x = np.asarray(inputs["L0_attn_normalized_resid_pre"], np.float32)
    WQ = np.asarray(inputs["L0_attn_W_Q"], np.float32)
    bQ = np.asarray(inputs["L0_attn_b_Q"], np.float32)
    WK = np.asarray(inputs["L0_attn_W_K"], np.float32)
    bK = np.asarray(inputs["L0_attn_b_K"], np.float32)
    WV = np.asarray(inputs["L0_attn_W_V"], np.float32)
    bV = np.asarray(inputs["L0_attn_b_V"], np.float32)
    WO = np.asarray(inputs["L0_attn_W_O"], np.float32)
    bO = np.asarray(inputs["L0_attn_b_O"], np.float32)
    wg = np.asarray(inputs["L0_ffn_w_gate"], np.float32)
    wu = np.asarray(inputs["L0_ffn_w_up"], np.float32)
    wd = np.asarray(inputs["L0_ffn_w_down"], np.float32)

    rp = _rope_perm()
    fp = _ffn_perm()

    inv = 10000.0 ** (-np.arange(64, dtype=np.float64) * 2.0 / 128.0)
    ang = np.outer(inv, np.arange(T, dtype=np.float64))
    cosv = np.cos(ang)
    sinv = np.sin(ang)
    ctab = np.concatenate([cosv, cosv], axis=0).astype(np.float32)
    stab = np.concatenate([-sinv, sinv], axis=0).astype(np.float32)

    swap = np.zeros((128, 128), np.float32)
    swap[np.arange(64), np.arange(64) + 64] = 1.0
    swap[np.arange(64) + 64, np.arange(64)] = 1.0

    maskt = np.zeros((128, 2048), np.float32)
    for i in range(4):
        pidx = np.arange(128)[:, None]
        cidx = np.arange(512)[None, :]
        maskt[:, i * 512:(i + 1) * 512] = (i * 128 + pidx <= cidx).astype(np.float32)
    maskt = maskt.astype(ml_dtypes.bfloat16)

    identm = np.eye(128, dtype=np.float32)

    wg_p = wg[:, fp]
    wu_p = wu[:, fp]
    wd_p = wd[fp, :]
    wg_t = np.ascontiguousarray(wg_p.reshape(16, 128, 64, 128).transpose(2, 1, 0, 3).reshape(64, 128, 2048))
    wu_t = np.ascontiguousarray(wu_p.reshape(16, 128, 64, 128).transpose(2, 1, 0, 3).reshape(64, 128, 2048))
    wd_t = np.ascontiguousarray(wd_p.reshape(64, 128, 2048))

    maps = []
    for core in range(8):
        b, half = core // 2, core % 2
        h0 = half * 8
        wq_c = WQ[h0:h0 + 8][:, :, rp]
        wk_c = WK[h0:h0 + 8][:, :, rp]
        wq_t = np.ascontiguousarray(wq_c.reshape(8, 16, 128, 128).transpose(0, 2, 1, 3).reshape(8, 128, 2048))
        wk_t = np.ascontiguousarray(wk_c.reshape(8, 16, 128, 128).transpose(0, 2, 1, 3).reshape(8, 128, 2048))
        wv_c = WV[h0:h0 + 8]
        wv_t = np.ascontiguousarray(wv_c.transpose(1, 0, 2).reshape(2048, 1024)
                                    .reshape(16, 128, 1024).transpose(1, 0, 2).reshape(128, 16 * 1024))
        wo_c = WO[h0:h0 + 8].reshape(1024, 2048)
        wo_t = np.ascontiguousarray(wo_c.reshape(8, 128, 2048).transpose(1, 0, 2).reshape(128, 8 * 2048))
        bqk_c = np.stack([bQ[h0 + i][rp] for i in range(8)] + [bK[h0 + i][rp] for i in range(8)],
                         axis=1).astype(np.float32)
        bv_c = bV[h0:h0 + 8].reshape(1, 1024).astype(np.float32)
        bo_c = (bO.reshape(1, D) if half == 0 else np.zeros((1, D), np.float32))
        maps.append({
            "x": np.ascontiguousarray(x[b]),
            "x_own": np.ascontiguousarray(x[b, half * 2048:(half + 1) * 2048]),
            "wq": wq_t, "wk": wk_t, "wv": wv_t, "wo": wo_t,
            "bqk": bqk_c, "bvrow": bv_c, "borow": bo_c,
            "wg": wg_t, "wu": wu_t, "wd": wd_t,
            "ctab": ctab, "stab": stab, "mask_t": maskt, "ident": identm,
            "swapm": swap.astype(ml_dtypes.bfloat16),
            "own0": np.array([[half * 2048, 2048 - half * 2048]], np.int32),
        })
    return maps


_NC_CACHE = {}


def kernel(**inputs):
    key = tuple(_DEBUG_TAPS)
    if key not in _NC_CACHE:
        _NC_CACHE[key] = build_nc()
    nc = _NC_CACHE[key]
    maps = _prep_inputs(inputs)
    res = run_bass_kernel_spmd(nc, maps, core_ids=list(range(8)))
    out = np.empty((4, T, D), np.float32)
    for c in range(8):
        b, half = c // 2, c % 2
        out[b, half * 2048:(half + 1) * 2048, :] = np.asarray(res.results[c]["out"])
    if _DEBUG_TAPS:
        kernel.last_results = res
    return out


# version 12 (final)
# speedup vs baseline: 1.1252x; 1.1252x over previous
import numpy as np
import ml_dtypes
from contextlib import ExitStack

import concourse.bass as bass
import concourse.mybir as mybir
from concourse.bass_utils import run_bass_kernel_spmd

F32 = mybir.dt.float32
BF16 = mybir.dt.bfloat16
I32 = mybir.dt.int32
AF = mybir.ActivationFunctionType
ALU = mybir.AluOpType
AX = mybir.AxisListType

T = 4096
D = 2048
HN = 8
DH = 128
II = 8192
TOK = 2048
GAMMA = 1.0 / float(np.sqrt(128.0))
EPS = 1e-5

_DEBUG_TAPS = []


class Sem:
    def __init__(self, h):
        self.h = h
        self.n = 0

    def inc(self, k=1):
        self.n += k
        return (self.h, k)

    def at(self):
        return (self.h, self.n)


class Plan:
    def __init__(self):
        self.items = []

    def op(self, key, fn, waits=(), inc=None):
        flat = []
        for w in waits:
            if w is None:
                continue
            if isinstance(w, list):
                flat.extend(x for x in w if x is not None)
            else:
                flat.append(w)
        self.items.append((key, fn, tuple(flat), inc))

    def replay(self, key, eng):
        seen = {}
        for k, fn, waits, inc in self.items:
            if k != key:
                continue
            for (h, v) in waits:
                hid = id(h)
                if seen.get(hid, -1) < v:
                    eng.wait_ge(h, v)
                    seen[hid] = v
            r = fn(eng)
            if inc is not None:
                r.then_inc(*inc)


class DmaQ:

    def __init__(self, plan, key, sems):
        self.P = plan
        self.key = key
        self.sems = sems
        self.g = -1
        self.first = False
        self.gate = None

    def group(self):
        self.g += 1
        s = self.sems[self.g % len(self.sems)]
        self.gate = (s.h, s.n)
        self.first = True

    def dma(self, fn, waits=()):
        s = self.sems[self.g % len(self.sems)]
        w = list(waits)
        if self.first:
            w.append(self.gate)
            self.first = False
        self.P.op(self.key, fn, waits=w, inc=s.inc(16))

    def done(self):
        s = self.sems[self.g % len(self.sems)]
        return (s.h, s.n)

    def all_done(self):
        return [(s.h, s.n) for s in self.sems]


def build_nc():
    nc = bass.Bass("TRN2", target_bir_lowering=False)

    d_x = nc.dram_tensor("x", [T, D], F32, kind="ExternalInput")
    d_wq = nc.dram_tensor("wq", [HN, 128, 2048], F32, kind="ExternalInput")
    d_wk = nc.dram_tensor("wk", [HN, 128, 2048], F32, kind="ExternalInput")
    d_wv = nc.dram_tensor("wv", [128, 16 * 1024], F32, kind="ExternalInput")
    d_wo = nc.dram_tensor("wo", [128, HN * 2048], F32, kind="ExternalInput")
    d_bqk = nc.dram_tensor("bqk", [128, 16], F32, kind="ExternalInput")
    d_bvrow = nc.dram_tensor("bvrow", [1, HN * DH], F32, kind="ExternalInput")
    d_borow = nc.dram_tensor("borow", [1, D], F32, kind="ExternalInput")
    d_wg = nc.dram_tensor("wg", [64, 128, 2048], F32, kind="ExternalInput")
    d_wu = nc.dram_tensor("wu", [64, 128, 2048], F32, kind="ExternalInput")
    d_wd = nc.dram_tensor("wd", [64, 128, 2048], F32, kind="ExternalInput")
    d_cs = nc.dram_tensor("ctab", [128, T], F32, kind="ExternalInput")
    d_sn = nc.dram_tensor("stab", [128, T], F32, kind="ExternalInput")
    d_mask = nc.dram_tensor("mask_t", [128, 2048], BF16, kind="ExternalInput")
    d_ident = nc.dram_tensor("ident", [128, 128], F32, kind="ExternalInput")
    d_swap = nc.dram_tensor("swapm", [128, 128], BF16, kind="ExternalInput")
    d_own0 = nc.dram_tensor("own0", [1, 2], I32, kind="ExternalInput")
    d_xo = nc.dram_tensor("x_own", [TOK, D], F32, kind="ExternalInput")

    d_out = nc.dram_tensor("out", [TOK, D], F32, kind="ExternalOutput")

    s_xn = nc.dram_tensor("s_xn", [T, D], BF16, kind="Internal")
    s_xnh = nc.dram_tensor("s_xnh", [TOK, D], BF16, kind="Internal")
    s_kT = nc.dram_tensor("s_kT", [HN, 128, T], BF16, kind="Internal")
    s_qT = nc.dram_tensor("s_qT", [HN, 128, T], BF16, kind="Internal")
    s_v = nc.dram_tensor("s_v", [T, HN * DH], BF16, kind="Internal")
    s_wd = nc.dram_tensor("s_wd", [64, 128, 2048], BF16, kind="Internal")
    s_hsp = nc.dram_tensor("s_hsp", [II, TOK], BF16, kind="Internal")
    s_ap = nc.dram_tensor("s_ap", [T, D], BF16, kind="Internal")
    s_ar = nc.dram_tensor("s_ar", [TOK, D], BF16, kind="Internal")
    s_ffn = nc.dram_tensor("s_ffn", [TOK, D], BF16, kind="Internal")

    taps = {}
    tap_shapes = {"xn": [T, D], "kT": [HN, 128, T], "q": [HN, 128, T], "v": [T, HN * DH],
                  "z": [128, HN * T], "ap": [T, D], "hsp": [II, TOK], "ffn": [TOK, D]}
    for name in _DEBUG_TAPS:
        taps[name] = nc.dram_tensor("tap_" + name, tap_shapes[name], BF16, kind="ExternalOutput")

    es = ExitStack()
    sb = lambda n, s, d: es.enter_context(nc.sbuf_tensor(n, s, d))
    pt = lambda n, s, d: es.enter_context(nc.psum_tensor(n, s, d))

    misc = sb("misc", [128, 1536], F32)
    ones_bc = sb("ones_bc", [128, 1], BF16)
    ones_br = sb("ones_br", [1, 128], BF16)
    ones_fr = sb("ones_fr", [1, 128], F32)
    ones_fc = sb("ones_fc", [128, 1], F32)
    ident = sb("identsb", [128, 128], F32)
    swapm = sb("swapsb", [128, 128], BF16)
    bqk = sb("bqksb", [128, 16], F32)
    bvrow = sb("bvrowsb", [1, HN * DH], BF16)
    borow = sb("borowsb", [1, D], BF16)
    kbarb_s = [sb("kbarbsb0", [128, 1], BF16), sb("kbarbsb1", [128, 1], BF16)]
    own0sb = sb("own0sb", [1, 2], I32)


    mc_sums = misc[0:1, 0:512]
    mc_recip = misc[0:1, 512:1024]
    mc_nk = misc[:, 1024:1056]
    mc_dk = misc[:, 1056:1088]
    mc_nq = misc[:, 1088:1120]
    mc_dq = misc[:, 1120:1152]
    mc_r2 = misc[:, 1152:1184]
    mc_qn = misc[:, 1184:1216]
    mc_bq = misc[:, 1216:1248]
    mc_ksum_s = [misc[:, 1248:1249], misc[:, 1260:1261]]
    mc_kbarsq_s = [misc[:, 1249:1250], misc[:, 1261:1262]]
    mc_ksb = misc[:, 1250:1251]
    mc_mr = misc[:, 1251:1252]
    mc_negB_s = [misc[:, 1252:1253], misc[:, 1262:1263]]
    mc_rmax = misc[:, 1253:1254]
    mc_t11a = misc[0:1, 1254:1255]
    mc_t11b = misc[0:1, 1255:1256]

    def ln_slot(i):
        base = 1280 + (i % 4) * 8
        return misc[:, base:base + 8]

    ABYTES = 192 * 1024
    AB = sb("arena", [128, ABYTES // 2], BF16)

    def c_bf(off_b, n, p0=0, p1=128):
        assert off_b % 2 == 0 and off_b + 2 * n <= ABYTES, (off_b, n)
        return AB[p0:p1, off_b // 2: off_b // 2 + n]

    def c_f32(off_b, n, p0=0, p1=128):
        assert off_b % 4 == 0 and off_b + 4 * n <= ABYTES, (off_b, n)
        return AB[p0:p1, off_b // 2: off_b // 2 + 2 * n].bitcast(F32)

    pA = pt("pA", [128, 1024], F32)
    pB = pt("pB", [128, 1024], F32)
    p4 = pt("p4", [128, 512], F32)
    p5 = pt("p5", [128, 512], F32)
    p6 = pt("p6", [128, 512], F32)
    p7 = pt("p7", [128, 512], F32)

    sP = Sem(es.enter_context(nc.semaphore("sP")))
    sA = Sem(es.enter_context(nc.semaphore("sA")))
    sV = Sem(es.enter_context(nc.semaphore("sV")))
    sG = Sem(es.enter_context(nc.semaphore("sG")))
    sC = Sem(es.enter_context(nc.semaphore("sC")))

    P = Plan()
    free = {}

    qS = DmaQ(P, "sync", [Sem(es.enter_context(nc.semaphore(f"qS{i}"))) for i in range(4)])
    qA = DmaQ(P, "scalar", [Sem(es.enter_context(nc.semaphore(f"qA{i}"))) for i in range(4)])
    qG = DmaQ(P, "gpsimd", [Sem(es.enter_context(nc.semaphore(f"qG{i}"))) for i in range(4)])

    P.op("vector", lambda e: e.memset(ones_bc[:, :], 1.0))
    P.op("vector", lambda e: e.memset(ones_br[:, :], 1.0))
    P.op("vector", lambda e: e.memset(ones_fr[:, :], 1.0))
    P.op("vector", lambda e: e.memset(ones_fc[:, :], 1.0), inc=sV.inc())
    v_ones = sV.at()

    qG.group()
    for dst, src in [(ident[:, :], d_ident[:, :]), (swapm[:, :], d_swap[:, :]),
                     (bqk[:, :], d_bqk[:, :]), (bvrow[:, :], d_bvrow[:, :]),
                     (borow[:, :], d_borow[:, :]),
                     (own0sb[:, :], d_own0[:, :])]:
        qG.dma(lambda e, dst=dst, src=src: e.dma_start(out=dst, in_=src))
    d_const = qG.done()

    a_cs = c_bf(0, T)
    a_sn = c_bf(8192, T)
    a_wqb = c_bf(16384, HN * 2048)
    a_wkb = c_bf(49152, HN * 2048)
    a_wvb = c_bf(81920, 16 * 1024)
    a_xt = [c_f32(114688 + 8192 * i, 2048) for i in range(2)]
    a_sq = c_f32(131072, 2048)
    a_xnT2 = [c_bf(139264 + 16384 * i, 16 * 512) for i in range(2)]
    a_xnt = [c_bf(172032 + 4096 * i, 2048) for i in range(2)]
    a_kbr = [c_bf(180224 + 2048 * i, 1024) for i in range(2)]
    a_rtp2 = c_f32(184320, 1024)
    a_kro = [c_bf(188416 + 2048 * i, 1024) for i in range(2)]
    a_vst = [c_bf(192512 + 2048 * i, 1024) for i in range(2)]

    qG.group()
    qG.dma(lambda e: e.dma_start(out=a_cs, in_=d_cs[:, :]))
    qG.dma(lambda e: e.dma_start(out=a_sn, in_=d_sn[:, :]))
    for h in range(HN):
        qG.dma(lambda e, h=h: e.dma_start(out=a_wqb[:, h * 2048:(h + 1) * 2048], in_=d_wq[h, :, :]))
        qG.dma(lambda e, h=h: e.dma_start(out=a_wkb[:, h * 2048:(h + 1) * 2048], in_=d_wk[h, :, :]))
    qG.dma(lambda e: e.dma_start(out=a_wvb, in_=d_wv[:, :]))
    d_wload = qG.done()

    xn_written = {}

    def emit_ln_tile(slot_idx, src_ap, dst_ap, tap_ap=None):
        buf = slot_idx % 2
        st = ln_slot(slot_idx)
        qS.group()
        qS.dma(lambda e, src_ap=src_ap, buf=buf: e.dma_start(out=a_xt[buf], in_=src_ap),
               waits=[free.get(("xt", buf))])
        m_x = qS.done()
        P.op("vector", lambda e, st=st, buf=buf: e.reduce_sum(st[:, 0:1], a_xt[buf], axis=AX.X),
             waits=[m_x], inc=sV.inc())
        P.op("vector", lambda e, st=st: e.tensor_scalar(out=st[:, 1:2], in0=st[:, 0:1],
                                                        scalar1=-1.0 / 2048.0, scalar2=None, op0=ALU.mult),
             waits=[sV.at()], inc=sV.inc())
        m_negm = sV.at()
        P.op("scalar", lambda e, st=st, buf=buf: e.activation(out=a_sq, in_=a_xt[buf], func=AF.Square,
                                                              bias=st[:, 1:2], scale=1.0,
                                                              accum_out=st[:, 2:3]),
             waits=[m_negm], inc=sA.inc())
        P.op("vector", lambda e, st=st: e.tensor_scalar(out=st[:, 3:4], in0=st[:, 2:3],
                                                        scalar1=1.0 / 2048.0, scalar2=EPS,
                                                        op0=ALU.mult, op1=ALU.add),
             waits=[sA.at()], inc=sV.inc())
        P.op("scalar", lambda e, st=st: e.activation(out=st[:, 4:5], in_=st[:, 3:4], func=AF.Sqrt),
             waits=[sV.at()], inc=sA.inc())
        P.op("vector", lambda e, st=st: e.reciprocal(st[:, 5:6], st[:, 4:5]),
             waits=[sA.at()], inc=sV.inc())
        P.op("vector", lambda e, st=st: e.tensor_scalar(out=st[:, 6:7], in0=st[:, 1:2],
                                                        scalar1=st[:, 5:6], scalar2=None, op0=ALU.mult),
             waits=[sV.at()], inc=sV.inc())
        m_rstd = sV.at()
        P.op("scalar", lambda e, st=st, buf=buf: e.activation(out=a_xnt[buf], in_=a_xt[buf],
                                                              func=AF.Identity, bias=st[:, 6:7],
                                                              scale=st[:, 5:6]),
             waits=[m_rstd, free.get(("xnt", buf))], inc=sA.inc())
        free[("xt", buf)] = sA.at()
        qS.group()
        qS.dma(lambda e, dst_ap=dst_ap, buf=buf: e.dma_start(out=dst_ap, in_=a_xnt[buf]),
               waits=[sA.at(), d_const])
        if tap_ap is not None:
            qS.dma(lambda e, tap_ap=tap_ap, buf=buf: e.dma_start(out=tap_ap, in_=a_xnt[buf]))
        mark = qS.done()
        free[("xnt", buf)] = mark
        return mark

    def emit_ln_pair(wp2):
        for s2 in range(8):
            ti = wp2 * 8 + s2
            xn_written[ti] = emit_ln_tile(
                ti, d_x[ti * 128:(ti + 1) * 128, :], s_xn[ti * 128:(ti + 1) * 128, :],
                taps["xn"][ti * 128:(ti + 1) * 128, :] if "xn" in taps else None)

    def emit_ln_own(wp2):
        for t2 in range(4 * wp2, 4 * wp2 + 4):
            emit_ln_tile(32 + t2, d_xo[t2 * 128:(t2 + 1) * 128, :], s_xnh[t2 * 128:(t2 + 1) * 128, :])

    emit_ln_pair(0)
    for wp in range(4):
        w0 = 2 * wp

        m_xnT_sub = ([], [])
        for sg in range(4):
            for half in range(2):
                w = w0 + half
                qA.group()
                for jc in range(sg * 4, sg * 4 + 4):
                    qA.dma(lambda e, w=w, jc=jc, half=half: e.dma_start(
                           out=a_xnT2[half][:, jc * 512:(jc + 1) * 512],
                           in_=s_xn[w * 512:(w + 1) * 512, jc * 128:(jc + 1) * 128],
                           transpose=True),
                           waits=[[xn_written[w * 4 + s2] for s2 in range(4)],
                                  free.get(("xnT", half))])
                m_xnT_sub[half].append(qA.done())
        m_xnT_a = m_xnT_sub[0][3]
        m_xnT_b = m_xnT_sub[1][3]

        kq_ps = {0: [p4, p5], 1: [p6, p7]}
        for h in range(HN):
            for which in range(2):
                wb = a_wkb if which == 0 else a_wqb
                pa_, pb_ = kq_ps[which]
                pname = ("kqp", which)
                scr = s_kT if which == 0 else s_qT
                tap = taps.get("kT" if which == 0 else "q")
                bias_col = bqk[:, 8 + h:9 + h] if which == 0 else bqk[:, h:h + 1]
                for jc in range(16):
                    P.op("tensor",
                         lambda e, wb=wb, pa_=pa_, h=h, jc=jc: e.matmul(
                             pa_[:, :], lhsT=wb[:, h * 2048 + jc * 128: h * 2048 + (jc + 1) * 128],
                             rhs=a_xnT2[0][:, jc * 512:(jc + 1) * 512], start=(jc == 0), stop=(jc == 15)),
                         waits=([m_xnT_sub[0][0], d_wload, d_const, v_ones, free.get(pname)]
                                if jc == 0 else
                                ([m_xnT_sub[0][jc // 4]] if jc % 4 == 0 else [])))
                    P.op("tensor",
                         lambda e, wb=wb, pb_=pb_, h=h, jc=jc: e.matmul(
                             pb_[:, :], lhsT=wb[:, h * 2048 + jc * 128: h * 2048 + (jc + 1) * 128],
                             rhs=a_xnT2[1][:, jc * 512:(jc + 1) * 512], start=(jc == 0), stop=(jc == 15)),
                         waits=([m_xnT_sub[1][jc // 4]] if jc % 4 == 0 else []),
                         inc=(sP.inc() if jc == 15 else None))
                m_proj = sP.at()
                kb = which
                P.op("scalar",
                     lambda e, pa_=pa_, bias_col=bias_col, kb=kb: e.activation(
                         out=a_kbr[kb][:, 0:512], in_=pa_[:, :], func=AF.Identity, bias=bias_col, scale=1.0),
                     waits=[m_proj, free.get(("kbr", kb))], inc=sA.inc())
                P.op("scalar",
                     lambda e, pb_=pb_, bias_col=bias_col, kb=kb: e.activation(
                         out=a_kbr[kb][:, 512:1024], in_=pb_[:, :], func=AF.Identity, bias=bias_col,
                         scale=1.0), inc=sA.inc())
                m_evac = sA.at()
                P.op("tensor",
                     lambda e, pa_=pa_, kb=kb: e.matmul(pa_[:, :], lhsT=swapm[:, :],
                                                        rhs=a_kbr[kb][:, 0:512], start=True, stop=True),
                     waits=[m_evac])
                P.op("tensor",
                     lambda e, pb_=pb_, kb=kb: e.matmul(pb_[:, :], lhsT=swapm[:, :],
                                                        rhs=a_kbr[kb][:, 512:1024], start=True, stop=True),
                     inc=sP.inc())
                m_swap = sP.at()
                cs = a_cs[:, w0 * 512: w0 * 512 + 1024]
                sn0 = a_sn[:, w0 * 512: w0 * 512 + 512]
                sn1 = a_sn[:, w0 * 512 + 512: w0 * 512 + 1024]
                kro = a_kro[kb]
                P.op("gpsimd", lambda e, kb=kb, cs=cs: e.tensor_tensor(out=a_rtp2, in0=a_kbr[kb],
                                                                       in1=cs, op=ALU.mult),
                     waits=[m_evac, free.get("rtp")], inc=sG.inc())
                m_t1 = sG.at()
                P.op("vector",
                     lambda e, pa_=pa_, sn0=sn0, kro=kro: e.scalar_tensor_tensor(
                         out=kro[:, 0:512], in0=pa_[:, :], scalar=1.0, in1=sn0, op0=ALU.mult,
                         op1=ALU.mult),
                     waits=[m_swap, free.get(("kro", kb))], inc=sV.inc())
                P.op("vector",
                     lambda e, pb_=pb_, sn1=sn1, kro=kro: e.scalar_tensor_tensor(
                         out=kro[:, 512:1024], in0=pb_[:, :], scalar=1.0, in1=sn1, op0=ALU.mult,
                         op1=ALU.mult), inc=sV.inc())
                P.op("vector", lambda e, kro=kro: e.tensor_tensor(out=kro, in0=kro, in1=a_rtp2,
                                                                  op=ALU.add),
                     waits=[m_t1, sV.at()], inc=sV.inc())
                m_rope = sV.at()
                free[("kbr", kb)] = m_t1
                free["rtp"] = m_rope
                free[pname] = sV.at()
                qS.group()
                qS.dma(lambda e, scr=scr, h=h, w0=w0, kro=kro: e.dma_start(
                       out=scr[h, :, w0 * 512: w0 * 512 + 1024], in_=kro),
                       waits=[m_rope])
                if tap is not None:
                    qS.dma(lambda e, tap=tap, h=h, w0=w0, kro=kro: e.dma_start(
                           out=tap[h, :, w0 * 512: w0 * 512 + 1024], in_=kro))
                free[("kro", kb)] = qS.done()

        if wp + 1 < 4:
            emit_ln_pair(wp + 1)
        emit_ln_own(wp)

        v_ps = [pA[:, 0:512], pA[:, 512:1024], pB[:, 0:512], pB[:, 512:1024]]
        for s in range(8):
            half = s // 4
            m_xnT_h = m_xnT_a if half == 0 else m_xnT_b
            b0 = 2 * (s % 2)
            pv0, pv1 = v_ps[b0], v_ps[b0 + 1]
            for jc in range(16):
                P.op("tensor",
                     lambda e, pv0=pv0, s=s, jc=jc, half=half: e.matmul(
                         pv0, lhsT=a_xnT2[half][:, jc * 512 + (s % 4) * 128: jc * 512 + (s % 4) * 128 + 128],
                         rhs=a_wvb[:, jc * 1024: jc * 1024 + 512],
                         start=(jc == 0), stop=False),
                     waits=([m_xnT_h, free.get(("vps", b0))] if jc == 0 else []))
                P.op("tensor",
                     lambda e, pv1=pv1, s=s, jc=jc, half=half: e.matmul(
                         pv1, lhsT=a_xnT2[half][:, jc * 512 + (s % 4) * 128: jc * 512 + (s % 4) * 128 + 128],
                         rhs=a_wvb[:, jc * 1024 + 512: jc * 1024 + 1024],
                         start=(jc == 0), stop=False),
                     waits=([free.get(("vps", b0 + 1))] if jc == 0 else []))
            P.op("tensor",
                 lambda e, pv0=pv0: e.matmul(pv0, lhsT=ones_br[0:1, :], rhs=bvrow[0:1, 0:512],
                                             start=False, stop=True))
            P.op("tensor",
                 lambda e, pv1=pv1: e.matmul(pv1, lhsT=ones_br[0:1, :], rhs=bvrow[0:1, 512:1024],
                                             start=False, stop=True),
                 inc=sP.inc())
            m_vp = sP.at()
            P.op("scalar",
                 lambda e, pv0=pv0, s=s: e.activation(out=a_vst[s % 2][:, 0:512], in_=pv0,
                                                      func=AF.Identity, scale=1.0),
                 waits=[m_vp, free.get(("vst", s % 2))], inc=sA.inc())
            free[("vps", b0)] = sA.at()
            P.op("scalar",
                 lambda e, pv1=pv1, s=s: e.activation(out=a_vst[s % 2][:, 512:1024], in_=pv1,
                                                      func=AF.Identity, scale=1.0), inc=sA.inc())
            free[("vps", b0 + 1)] = sA.at()
            m_vst = sA.at()
            qS.group()
            qS.dma(lambda e, w0=w0, s=s: e.dma_start(
                   out=s_v[w0 * 512 + s * 128: w0 * 512 + (s + 1) * 128, :], in_=a_vst[s % 2]),
                   waits=[m_vst])
            if "v" in taps:
                qS.dma(lambda e, w0=w0, s=s: e.dma_start(
                       out=taps["v"][w0 * 512 + s * 128: w0 * 512 + (s + 1) * 128, :], in_=a_vst[s % 2]))
            free[("vst", s % 2)] = qS.done()
            if s == 3:
                free[("xnT", 0)] = sP.at()
        free[("xnT", 1)] = sP.at()
    xnh_written = qS.all_done()

    mB_pe = sP.at()
    mB_act = sA.at()
    mB_dve = sV.at()
    mB_pool = sG.at()
    mB_sync = qS.all_done()

    a_vb = c_bf(0, 32 * 1024)
    a_zb = c_bf(65536, HN * T)
    KQ0 = 131072
    a_kq = [(c_bf(KQ0 + 16384 * i, T), c_bf(KQ0 + 16384 * i + 8192, T)) for i in range(2)]
    a_pT = c_bf(163840, 4 * 512)
    a_ktmp = c_bf(172032, T)
    a_qtmp = c_bf(180224, T)
    a_recipb = c_f32(190464, 512)
    a_masks = [c_bf(188416, 1024), c_bf(192512, 1024)]

    mB_all = [mB_pe, mB_act, mB_dve, mB_pool, mB_sync, qA.all_done()]
    qS.group()
    qS.dma(lambda e: e.dma_start(out=a_vb, in_=s_v.rearrange("(w p) d -> p w d", p=128)),
           waits=mB_all)
    m_vb = qS.done()
    qG.group()
    qG.dma(lambda e: e.dma_start(out=a_masks[0], in_=d_mask[:, 0:1024]), waits=mB_all)
    qG.dma(lambda e: e.dma_start(out=a_masks[1], in_=d_mask[:, 1024:2048]))
    d_masks = qG.done()
    kq_loaded = {}
    for h in range(2):
        qS.group()
        qS.dma(lambda e, h=h: e.dma_start(out=a_kq[h % 2][0], in_=s_kT[h, :, :]), waits=mB_all)
        qS.dma(lambda e, h=h: e.dma_start(out=a_kq[h % 2][1], in_=s_qT[h, :, :]))
        kq_loaded[h] = qS.done()

    a_wdc = [c_bf(163840 + 4096 + 2048 * i, 1024) for i in range(2)]
    for ic in range(64):
        for hf in range(2):
            j = (ic * 2 + hf) % 2
            qG.group()
            qG.dma(lambda e, ic=ic, hf=hf, j=j: e.dma_start(out=a_wdc[j],
                                                            in_=d_wd[ic, :, hf * 1024:(hf + 1) * 1024]),
                   waits=[mB_pool if ic == 0 else None, mB_pe if ic == 0 else None,
                          free.get(("wdc", j))])
            m_ld = qG.done()
            qG.group()
            qG.dma(lambda e, ic=ic, hf=hf, j=j: e.dma_start(out=s_wd[ic, :, hf * 1024:(hf + 1) * 1024],
                                                            in_=a_wdc[j]),
                   waits=[m_ld])
            free[("wdc", j)] = qG.done()
    d_wdcast = qG.all_done()

    m_lastexp = {}
    pending_norm = []

    def flush_norm():
        if not pending_norm:
            return
        hh_, gg_, pz_, m_mm = pending_norm.pop()
        P.op("vector", lambda e: e.tensor_scalar(out=mc_sums, in0=p4[0:1, :], scalar1=1e-30, scalar2=None,
                                                 op0=ALU.max),
             waits=[m_mm], inc=sV.inc())
        P.op("vector", lambda e: e.reciprocal(mc_recip, mc_sums), waits=[sV.at()], inc=sV.inc())
        P.op("tensor", lambda e: e.matmul(p4[:, 0:512], lhsT=ones_fr[0:1, :], rhs=mc_recip,
                                          start=True, stop=True),
             waits=[sV.at()], inc=sP.inc())
        P.op("vector", lambda e: e.tensor_copy(out=a_recipb, in_=p4[:, 0:512]),
             waits=[sP.at()], inc=sV.inc())
        free["p4sum"] = sV.at()
        P.op("vector",
             lambda e, hh_=hh_, gg_=gg_, pz_=pz_: e.tensor_tensor(
                 out=a_zb[:, hh_ * T + gg_ * 512: hh_ * T + (gg_ + 1) * 512],
                 in0=pz_[:, :], in1=a_recipb, op=ALU.mult),
             waits=[sV.at()], inc=sV.inc())
        free[("pz", gg_ % 2)] = sV.at()

    for h in range(HN):
        K_h, Q_h = a_kq[h % 2]
        mKQ = kq_loaded[h]
        mc_ksum = mc_ksum_s[h % 2]
        mc_kbarsq = mc_kbarsq_s[h % 2]
        mc_negB = mc_negB_s[h % 2]
        kbarb = kbarb_s[h % 2]
        P.op("scalar", lambda e, K_h=K_h: e.activation(out=a_ktmp, in_=K_h, func=AF.Square),
             waits=[mKQ, mB_all[1] if h == 0 else None, free.get("ktmp")], inc=sA.inc())
        m_ksq = sA.at()
        P.op("scalar", lambda e, Q_h=Q_h: e.activation(out=a_qtmp, in_=Q_h, func=AF.Square),
             waits=[free.get("qtmp")], inc=sA.inc())
        m_qsq = sA.at()
        P.op("vector", lambda e, K_h=K_h, mc_ksum=mc_ksum: e.reduce_sum(mc_ksum, K_h, axis=AX.X),
             waits=[mKQ, m_lastexp.get(h - 2)], inc=sV.inc())
        P.op("vector", lambda e, kbarb=kbarb, mc_ksum=mc_ksum: e.tensor_copy(out=kbarb[:, :], in_=mc_ksum),
             waits=[sV.at()], inc=sV.inc())
        m_kbar = sV.at()
        P.op("scalar", lambda e, mc_kbarsq=mc_kbarsq, mc_ksum=mc_ksum: e.activation(
             out=mc_kbarsq, in_=mc_ksum, func=AF.Square),
             waits=[m_kbar], inc=sA.inc())
        m_kbarsq = sA.at()
        for kc in range(32):
            P.op("tensor", lambda e, kc=kc: e.matmul(p7[:, kc:kc + 1],
                                                     lhsT=a_ktmp[:, kc * 128:(kc + 1) * 128], rhs=ones_bc[:, :],
                                                     start=True, stop=True),
                 waits=([m_ksq, free.get(("kqp", 1, 1)), free.get("p7")] if kc == 0 else []))
        for kc in range(32):
            P.op("tensor", lambda e, kc=kc, K_h=K_h, kbarb=kbarb: e.matmul(
                 p7[:, 32 + kc:33 + kc], lhsT=K_h[:, kc * 128:(kc + 1) * 128],
                 rhs=kbarb[:, :], start=True, stop=True),
                 waits=([m_kbar] if kc == 0 else []))
        for kc in range(32):
            P.op("tensor", lambda e, kc=kc: e.matmul(p7[:, 64 + kc:65 + kc],
                                                     lhsT=a_qtmp[:, kc * 128:(kc + 1) * 128], rhs=ones_bc[:, :],
                                                     start=True, stop=True),
                 waits=([m_qsq] if kc == 0 else []))
        for kc in range(32):
            P.op("tensor", lambda e, kc=kc, Q_h=Q_h, kbarb=kbarb: e.matmul(
                 p7[:, 96 + kc:97 + kc], lhsT=Q_h[:, kc * 128:(kc + 1) * 128],
                 rhs=kbarb[:, :], start=True, stop=True))
        P.op("tensor", lambda e, mc_kbarsq=mc_kbarsq: e.matmul(p7[0:1, 128:129], lhsT=mc_kbarsq,
                                                               rhs=ones_fc[:, :], start=True, stop=True),
             waits=[m_kbarsq], inc=sP.inc())
        m_stats = sP.at()
        free["ktmp"] = m_stats
        free["qtmp"] = m_stats
        P.op("vector", lambda e: e.tensor_copy(out=mc_nk, in_=p7[:, 0:32]), waits=[m_stats], inc=sV.inc())
        P.op("vector", lambda e: e.tensor_copy(out=mc_dk, in_=p7[:, 32:64]), inc=sV.inc())
        P.op("vector", lambda e: e.tensor_copy(out=mc_nq, in_=p7[:, 64:96]), inc=sV.inc())
        P.op("vector", lambda e: e.tensor_copy(out=mc_dq, in_=p7[:, 96:128]), inc=sV.inc())
        P.op("vector", lambda e: e.tensor_scalar(out=mc_t11a, in0=p7[0:1, 128:129],
                                                 scalar1=1.0 / (4096.0 * 4096.0), scalar2=None, op0=ALU.mult),
             inc=sV.inc())
        m_ev = sV.at()
        P.op("vector", lambda e: e.scalar_tensor_tensor(out=mc_r2, in0=mc_dk, scalar=-2.0 / 4096.0, in1=mc_nk,
                                                        op0=ALU.mult, op1=ALU.add),
             waits=[m_ev], inc=sV.inc())
        m_r2a = sV.at()
        P.op("tensor", lambda e: e.matmul(p7[:, 136:137], lhsT=ones_fr[0:1, :], rhs=mc_t11a,
                                          start=True, stop=True),
             waits=[m_ev], inc=sP.inc())
        P.op("vector", lambda e: e.tensor_copy(out=mc_ksb, in_=p7[:, 136:137]), waits=[sP.at()], inc=sV.inc())
        P.op("vector", lambda e: e.tensor_scalar(out=mc_r2, in0=mc_r2, scalar1=mc_ksb, scalar2=None, op0=ALU.add),
             waits=[sV.at(), m_r2a], inc=sV.inc())
        P.op("vector", lambda e: e.reduce_max(mc_rmax, mc_r2, axis=AX.X), waits=[sV.at()], inc=sV.inc())
        P.op("tensor", lambda e: e.transpose(p7[0:1, 144:272], mc_rmax, ident[:, :]),
             waits=[sV.at()], inc=sP.inc())
        P.op("vector", lambda e: e.reduce_max(mc_t11b, p7[0:1, 144:272], axis=AX.X),
             waits=[sP.at()], inc=sV.inc())
        P.op("scalar", lambda e: e.activation(out=mc_t11a, in_=mc_t11b, func=AF.Sqrt),
             waits=[sV.at()], inc=sA.inc())
        P.op("tensor", lambda e: e.matmul(p7[:, 136:137], lhsT=ones_fr[0:1, :], rhs=mc_t11a,
                                          start=True, stop=True),
             waits=[sA.at()], inc=sP.inc())
        P.op("vector", lambda e: e.tensor_scalar(out=mc_mr, in0=p7[:, 136:137], scalar1=GAMMA, scalar2=None,
                                                 op0=ALU.mult),
             waits=[sP.at()], inc=sV.inc())
        m_mr = sV.at()
        P.op("scalar", lambda e: e.activation(out=mc_qn, in_=mc_nq, func=AF.Sqrt),
             waits=[m_ev], inc=sA.inc())
        m_qn = sA.at()
        P.op("vector", lambda e: e.tensor_scalar(out=mc_dq, in0=mc_dq, scalar1=GAMMA / 4096.0, scalar2=None,
                                                 op0=ALU.mult),
             waits=[m_ev], inc=sV.inc())
        P.op("vector", lambda e: e.scalar_tensor_tensor(out=mc_bq, in0=mc_qn, scalar=mc_mr, in1=mc_dq,
                                                        op0=ALU.mult, op1=ALU.add),
             waits=[m_mr, m_qn, sV.at()], inc=sV.inc())
        P.op("vector", lambda e: e.reduce_max(mc_rmax, mc_bq, axis=AX.X), waits=[sV.at()], inc=sV.inc())
        P.op("tensor", lambda e: e.transpose(p7[0:1, 144:272], mc_rmax, ident[:, :]),
             waits=[sV.at()], inc=sP.inc())
        P.op("vector", lambda e: e.reduce_max(mc_t11b, p7[0:1, 144:272], axis=AX.X),
             waits=[sP.at()], inc=sV.inc())
        P.op("vector", lambda e: e.tensor_scalar(out=mc_t11a, in0=mc_t11b, scalar1=-1.0, scalar2=-1.0,
                                                 op0=ALU.mult, op1=ALU.add),
             waits=[sV.at()], inc=sV.inc())
        P.op("tensor", lambda e: e.matmul(p7[:, 136:137], lhsT=ones_fr[0:1, :], rhs=mc_t11a,
                                          start=True, stop=True),
             waits=[sV.at()], inc=sP.inc())
        P.op("vector", lambda e, mc_negB=mc_negB: e.tensor_copy(out=mc_negB, in_=p7[:, 136:137]),
             waits=[sP.at()], inc=sV.inc())
        m_negB = sV.at()
        free["p7"] = m_negB

        if 1 <= h < HN - 1:
            qS.group()
            qS.dma(lambda e, h=h: e.dma_start(out=a_kq[(h + 1) % 2][0], in_=s_kT[h + 1, :, :]),
                   waits=[free.get(("kqbuf", (h + 1) % 2))])
            qS.dma(lambda e, h=h: e.dma_start(out=a_kq[(h + 1) % 2][1], in_=s_qT[h + 1, :, :]))
            kq_loaded[h + 1] = qS.done()

        for g in range(8):
            KC = 4 * (g + 1)
            pairs = KC // 2
            pz = [p5, p6][g % 2]
            m_group_mm = None
            def emit_sz(pr, m_pr):
                kc0 = 2 * pr
                for q in range(2):
                    kc = kc0 + q
                    sl = ((kc % 4)) * 512
                    P.op("tensor",
                         lambda e, kc=kc, sl=sl, KC=KC: e.matmul(p4[0:1, :], lhsT=ones_bc[:, :],
                                                                 rhs=a_pT[:, sl:sl + 512],
                                                                 start=(kc == 0), stop=(kc == KC - 1)),
                         waits=[m_pr, free.get("p4sum") if kc == 0 else None])
                    P.op("tensor",
                         lambda e, kc=kc, sl=sl, h=h, pz=pz, KC=KC: e.matmul(
                             pz[:, :], lhsT=a_vb[:, kc * 1024 + h * 128: kc * 1024 + (h + 1) * 128],
                             rhs=a_pT[:, sl:sl + 512], start=(kc == 0), stop=(kc == KC - 1)),
                         waits=[free.get(("pz", g % 2)) if kc == 0 else None],
                         inc=(sP.inc() if kc == KC - 1 else None))

            pend_sz = None
            for pr in range(pairs):
                kc0 = 2 * pr
                psX = [pA, pB][pr % 2]
                P.op("tensor",
                     lambda e, psX=psX, K_h=K_h, Q_h=Q_h, kc0=kc0, g=g: e.matmul(
                         psX[:, 0:512], lhsT=K_h[:, kc0 * 128:(kc0 + 1) * 128],
                         rhs=Q_h[:, g * 512:(g + 1) * 512], start=True, stop=True),
                     waits=[m_vb if (h == 0 and g == 0 and pr == 0) else None,
                            free.get(("spair", pr % 2)), m_negB if pr == 0 else None])
                P.op("tensor",
                     lambda e, psX=psX, K_h=K_h, Q_h=Q_h, kc0=kc0, g=g: e.matmul(
                         psX[:, 512:1024], lhsT=K_h[:, (kc0 + 1) * 128:(kc0 + 2) * 128],
                         rhs=Q_h[:, g * 512:(g + 1) * 512], start=True, stop=True),
                     inc=sP.inc())
                m_spair = sP.at()
                if pr == 0:
                    flush_norm()
                if pend_sz is not None:
                    emit_sz(*pend_sz)
                slot = (kc0 % 4) * 512
                P.op("scalar",
                     lambda e, psX=psX, slot=slot, mc_negB=mc_negB: e.activation(
                         out=a_pT[:, slot:slot + 1024], in_=psX[:, :],
                         func=AF.Exp, bias=mc_negB, scale=GAMMA),
                     waits=[m_spair], inc=sA.inc())
                m_exp = sA.at()
                free[("spair", pr % 2)] = m_exp
                m_pr = m_exp
                if pr >= pairs - 2:
                    mi = pr - (pairs - 2)
                    P.op("vector",
                         lambda e, slot=slot, mi=mi: e.tensor_tensor(out=a_pT[:, slot:slot + 1024],
                                                                     in0=a_pT[:, slot:slot + 1024],
                                                                     in1=a_masks[mi], op=ALU.mult),
                         waits=[m_exp, d_masks], inc=sV.inc())
                    m_pr = sV.at()
                pend_sz = (pr, m_pr)
            emit_sz(*pend_sz)
            m_group_mm = sP.at()
            pending_norm.append((h, g, pz, m_group_mm))
        flush_norm()
        m_lastexp[h] = sA.at()
        free[("kqbuf", h % 2)] = sP.at()

    mD_pe = sP.at()
    mD_dve = sV.at()
    if "z" in taps:
        qS.group()
        qS.dma(lambda e: e.dma_start(out=taps["z"][:, :], in_=a_zb), waits=[mD_dve])

    a_wob = c_bf(KQ0, HN * 2048)
    a_est = [c_bf(163840 + 1024 * i, 512) for i in range(8)]

    qG.group()
    qG.dma(lambda e: e.dma_start(out=a_wob, in_=d_wo[:, :]), waits=[mD_pe, d_wdcast])
    m_wob = qG.done()

    a_xno = c_bf(0, 16 * 2048)
    qA.group()
    for jc in range(16):
        for tq in range(4):
            qA.dma(lambda e, jc=jc, tq=tq: e.dma_start(
                   out=a_xno[:, jc * 2048 + tq * 512: jc * 2048 + (tq + 1) * 512],
                   in_=s_xnh[tq * 512:(tq + 1) * 512, jc * 128:(jc + 1) * 128],
                   transpose=True),
                   waits=([mD_pe, mD_dve] + [xnh_written] if (jc == 0 and tq == 0) else []))
    m_xno_early = qA.done()

    e_ps2 = [[pA[:, 0:512], pA[:, 512:1024], pB[:, 0:512], pB[:, 512:1024]],
             [p4[:, :], p5[:, :], p6[:, :], p7[:, :]]]
    for qt in range(32):
        par = qt % 2
        banks = e_ps2[par]
        for mg in range(4):
            for hh in range(HN):
                P.op("tensor",
                     lambda e, banks=banks, qt=qt, mg=mg, hh=hh: e.matmul(
                         banks[mg], lhsT=a_zb[:, hh * T + qt * 128: hh * T + (qt + 1) * 128],
                         rhs=a_wob[:, hh * 2048 + mg * 512: hh * 2048 + (mg + 1) * 512],
                         start=(hh == 0), stop=False),
                     waits=([m_wob, mD_dve, mD_pe, free.get(("eps", par, mg))]
                            if hh == 0 else []))
        for mg in range(4):
            P.op("tensor",
                 lambda e, banks=banks, mg=mg: e.matmul(banks[mg], lhsT=ones_br[0:1, :],
                                                        rhs=borow[0:1, mg * 512:(mg + 1) * 512],
                                                        start=False, stop=True),
                 inc=(sP.inc() if mg == 3 else None))
        m_e = sP.at()
        qS.group()
        for mg in range(4):
            P.op("scalar", lambda e, banks=banks, mg=mg, par=par: e.activation(
                 out=a_est[4 * par + mg], in_=banks[mg], func=AF.Identity, scale=1.0),
                 waits=[m_e, free.get(("est", 4 * par + mg))], inc=sA.inc())
            m_ee = sA.at()
            free[("eps", par, mg)] = m_ee
            qS.dma(lambda e, qt=qt, mg=mg, par=par: e.dma_start(
                   out=s_ap[qt * 128:(qt + 1) * 128, mg * 512:(mg + 1) * 512],
                   in_=a_est[4 * par + mg]),
                   waits=[m_ee])
            if "ap" in taps:
                qS.dma(lambda e, qt=qt, mg=mg, par=par: e.dma_start(
                       out=taps["ap"][qt * 128:(qt + 1) * 128, mg * 512:(mg + 1) * 512],
                       in_=a_est[4 * par + mg]))
        m_apg = qS.done()
        for mg in range(4):
            free[("est", 4 * par + mg)] = m_apg
    m_ap = qS.all_done()

    P.op("gpsimd", lambda e: e.collective_compute(
        "ReduceScatter", ALU.add, replica_groups=[[0, 1], [2, 3], [4, 5], [6, 7]],
        ins=[s_ap.ap().opt()], outs=[s_ar.ap().opt()]),
        waits=[m_ap], inc=sC.inc())

    a_wgb = [c_bf(65536 + 4096 * i, 2048) for i in range(4)]
    a_wub = [c_bf(81920 + 4096 * i, 2048) for i in range(4)]
    a_wgb2 = [c_bf(98304 + 4096 * i, 2048) for i in range(4)]
    a_wub2 = [c_bf(114688 + 4096 * i, 2048) for i in range(4)]
    a_h = [c_f32(131072 + 2048 * i, 512) for i in range(16)]
    a_habs = [c_f32(163840 + 2048 * i, 512) for i in range(4)]
    a_tau = c_f32(172032, 512)
    a_msk = c_f32(174080, 512)
    a_gs = c_f32(176128, 512)
    a_hst = [c_bf(178176 + 1024 * i, 512) for i in range(4)]

    mE_done = [sP.at(), sA.at(), qS.all_done()]
    m_xno = m_xno_early

    wgsets = [(a_wgb, a_wub), (a_wgb2, a_wub2)]
    wg_loaded = {}

    def emit_wload(gc):
        s = gc % 2
        gset, uset = wgsets[s]
        qG.group()
        for j in range(4):
            it = j * 16 + gc
            qG.dma(lambda e, it=it, j=j, gset=gset: e.dma_start(out=gset[j], in_=d_wg[it, :, :]),
                   waits=[free.get(("wgs", s)), mE_done[0] if gc < 2 else None])
            qG.dma(lambda e, it=it, j=j, uset=uset: e.dma_start(out=uset[j], in_=d_wu[it, :, :]))
        wg_loaded[gc] = qG.done()

    emit_wload(0)
    emit_wload(1)

    g_ps = [pA[:, 0:512], pA[:, 512:1024], pB[:, 0:512], pB[:, 512:1024]]
    d_ps = [p4[:, :], p5[:, :], p6[:, :], p7[:, :]]
    for gc in range(16):
        s = gc % 2
        gset, uset = wgsets[s]
        for j in range(4):
            for tq in range(4):
                for jc in range(16):
                    P.op("tensor",
                         lambda e, gset=gset, j=j, jc=jc, tq=tq: e.matmul(
                             g_ps[tq], lhsT=gset[j][:, jc * 128:(jc + 1) * 128],
                             rhs=a_xno[:, jc * 2048 + tq * 512: jc * 2048 + (tq + 1) * 512],
                             start=(jc == 0), stop=(jc == 15)),
                         waits=([wg_loaded[gc], m_xno, free.get(("gps", tq))]
                                if (jc == 0 and tq == 0) else
                                ([free.get(("gps", tq))] if jc == 0 else [])),
                         inc=(sP.inc() if (jc == 15 and tq == 3) else None))
            m_g = sP.at()
            for tq in range(4):
                for jc in range(16):
                    P.op("tensor",
                         lambda e, uset=uset, j=j, jc=jc, tq=tq: e.matmul(
                             d_ps[tq], lhsT=uset[j][:, jc * 128:(jc + 1) * 128],
                             rhs=a_xno[:, jc * 2048 + tq * 512: jc * 2048 + (tq + 1) * 512],
                             start=(jc == 0), stop=(jc == 15)),
                         waits=([free.get(("dps", tq))] if jc == 0 else []),
                         inc=(sP.inc() if (jc == 15 and tq == 3) else None))
            m_u = sP.at()
            for tq in range(4):
                P.op("scalar", lambda e, tq=tq: e.activation(out=a_gs, in_=g_ps[tq], func=AF.Silu),
                     waits=[m_g, free.get("gs")], inc=sA.inc())
                m_silu = sA.at()
                free[("gps", tq)] = m_silu
                P.op("vector", lambda e, j=j, tq=tq: e.tensor_tensor(out=a_h[4 * j + tq], in0=d_ps[tq],
                                                                     in1=a_gs, op=ALU.mult),
                     waits=[m_u, m_silu, free.get(("hbuf", 4 * j + tq))], inc=sV.inc())
                free[("dps", tq)] = sV.at()
                free["gs"] = sV.at()
        for tq in range(4):
            for j in range(4):
                P.op("scalar", lambda e, j=j, tq=tq: e.activation(out=a_habs[j], in_=a_h[4 * j + tq],
                                                                  func=AF.Abs),
                     waits=[sV.at() if j == 0 and tq == 0 else None, free.get(("habs", j))], inc=sA.inc())
            m_habs = sA.at()
            P.op("vector", lambda e: e.tensor_tensor(out=a_tau, in0=a_habs[0], in1=a_habs[1], op=ALU.max),
                 waits=[m_habs, free.get("tau")], inc=sV.inc())
            P.op("vector", lambda e: e.tensor_tensor(out=a_msk, in0=a_habs[2], in1=a_habs[3], op=ALU.max),
                 waits=[free.get("msk")], inc=sV.inc())
            P.op("vector", lambda e: e.tensor_tensor(out=a_tau, in0=a_tau, in1=a_msk, op=ALU.min),
                 waits=[sV.at()], inc=sV.inc())
            P.op("vector", lambda e: e.tensor_tensor(out=a_gs, in0=a_habs[0], in1=a_habs[1], op=ALU.min),
                 inc=sV.inc())
            P.op("vector", lambda e: e.tensor_tensor(out=a_msk, in0=a_habs[2], in1=a_habs[3], op=ALU.min),
                 waits=[sV.at()], inc=sV.inc())
            P.op("vector", lambda e: e.tensor_tensor(out=a_gs, in0=a_gs, in1=a_msk, op=ALU.max),
                 waits=[sV.at()], inc=sV.inc())
            P.op("vector", lambda e: e.tensor_tensor(out=a_tau, in0=a_tau, in1=a_gs, op=ALU.max),
                 waits=[sV.at()], inc=sV.inc())
            m_tau = sV.at()
            free["msk"] = m_tau
            free["gs"] = m_tau
            for j in range(4):
                P.op("vector", lambda e, j=j: e.tensor_tensor(out=a_msk, in0=a_habs[j], in1=a_tau, op=ALU.is_ge),
                     waits=[m_tau, sV.at()], inc=sV.inc())
                free[("habs", j)] = sV.at()
                P.op("vector", lambda e, j=j, tq=tq: e.tensor_tensor(out=a_hst[j], in0=a_h[4 * j + tq],
                                                                      in1=a_msk, op=ALU.mult),
                     waits=[sV.at(), free.get(("hst", j))], inc=sV.inc())
                free[("hbuf", 4 * j + tq)] = sV.at()
                free["msk"] = sV.at()
                free["tau"] = sV.at()
                m_hs = sV.at()
                it = j * 16 + gc
                qS.group()
                qS.dma(lambda e, it=it, tq=tq, j=j: e.dma_start(
                       out=s_hsp[it * 128:(it + 1) * 128, tq * 512:(tq + 1) * 512], in_=a_hst[j]),
                       waits=[m_hs])
                if "hsp" in taps:
                    qS.dma(lambda e, it=it, tq=tq, j=j: e.dma_start(
                           out=taps["hsp"][it * 128:(it + 1) * 128, tq * 512:(tq + 1) * 512], in_=a_hst[j]))
                free[("hst", j)] = qS.done()
        free[("wgs", s)] = sP.at()
        if gc + 2 < 16:
            emit_wload(gc + 2)
    m_gate_pe = sP.at()
    m_gate_dve = sV.at()
    m_gate_act = sA.at()
    m_gate_sync = qS.all_done()

    a_hw = c_bf(0, 64 * 1024)
    a_wdt = [c_bf(131072 + 1024 * i, 512) for i in range(4)]
    a_fst = [c_bf(135168 + 1024 * i, 512) for i in range(8)]

    d_banks = [pA[:, 0:512], pA[:, 512:1024], pB[:, 0:512], pB[:, 512:1024],
               p4[:, :], p5[:, :], p6[:, :], p7[:, :]]

    a_gx = [c_f32(143360 + 8192 * i, 2048) for i in range(2)]
    a_ga = [c_bf(159744 + 4096 * i, 2048) for i in range(2)]
    a_gf = [c_bf(167936 + 4096 * i, 2048) for i in range(2)]
    a_go = [c_f32(176128 + 8192 * i, 2048) for i in range(2)]

    def emit_G_half(gh, th_done):
        for tt in range(8 * gh, 8 * gh + 8):
            b = tt % 2
            qS.group()
            qS.dma(lambda e, tt=tt, b=b: e.dma_start(out=a_gx[b], in_=d_xo[tt * 128:(tt + 1) * 128, :]),
                   waits=(th_done if tt % 8 < 2 else []) + [free.get(("gx", b))])
            m_gx = qS.done()
            qS.group()
            qS.dma(lambda e, tt=tt, b=b: e.dma_start(out=a_ga[b], in_=s_ar[tt * 128:(tt + 1) * 128, :]),
                   waits=[(sC.h, 1) if tt == 8 * gh else None, free.get(("ga", b))])
            m_ga = qS.done()
            qS.group()
            qS.dma(lambda e, tt=tt, b=b: e.dma_start(out=a_gf[b], in_=s_ffn[tt * 128:(tt + 1) * 128, :]),
                   waits=[free.get(("gf", b))])
            m_gf = qS.done()
            P.op("vector", lambda e, b=b: e.tensor_tensor(out=a_go[b], in0=a_ga[b], in1=a_gf[b], op=ALU.add),
                 waits=[m_ga, m_gf, free.get(("go", b))], inc=sV.inc())
            free[("ga", b)] = sV.at()
            free[("gf", b)] = sV.at()
            P.op("vector", lambda e, b=b: e.tensor_tensor(out=a_go[b], in0=a_go[b], in1=a_gx[b], op=ALU.add),
                 waits=[m_gx, sV.at()], inc=sV.inc())
            free[("gx", b)] = sV.at()
            m_go = sV.at()
            qS.group()
            qS.dma(lambda e, tt=tt, b=b: e.dma_start(out=d_out[tt * 128:(tt + 1) * 128, :], in_=a_go[b]),
                   waits=[m_go])
            free[("go", b)] = qS.done()

    for th in range(2):
        qS.group()
        qS.dma(lambda e, th=th: e.dma_start(
               out=a_hw, in_=s_hsp[:, th * 1024:(th + 1) * 1024].rearrange("(ic p) t -> p ic t", p=128)),
               waits=[m_gate_pe, m_gate_sync, free.get("hwin")])
        m_hw = qS.done()
        for mg in range(4):
            for ic in range(64):
                wsl = ic % 4
                qA.group()
                qA.dma(lambda e, ic=ic, mg=mg, wsl=wsl: e.dma_start(out=a_wdt[wsl],
                                                                    in_=s_wd[ic, :, mg * 512:(mg + 1) * 512]),
                       waits=[d_wdcast if (th == 0 and mg == 0 and ic < 4) else None,
                              m_gate_pe if (th == 0 and mg == 0 and ic == 0) else None,
                              m_gate_dve if (th == 0 and mg == 0 and ic == 0) else None,
                              m_gate_act if (th == 0 and mg == 0 and ic == 0) else None,
                              free.get(("wdt", wsl))])
                m_wd = qA.done()
                for tt in range(8):
                    w_list = [m_wd]
                    if ic == 0:
                        w_list += [m_hw, free.get(("dbk", tt)),
                                   free.get(("gps", tt)) if tt < 4 else free.get(("dps", tt - 4))]
                    P.op("tensor",
                         lambda e, ic=ic, tt=tt, wsl=wsl: e.matmul(
                             d_banks[tt], lhsT=a_hw[:, ic * 1024 + tt * 128: ic * 1024 + (tt + 1) * 128],
                             rhs=a_wdt[wsl], start=(ic == 0), stop=(ic == 63)),
                         waits=w_list,
                         inc=(sP.inc() if tt == 7 else None))
                free[("wdt", wsl)] = sP.at()
            m_dmm = sP.at()
            for tt in range(8):
                P.op("scalar", lambda e, tt=tt: e.activation(out=a_fst[tt], in_=d_banks[tt],
                                                             func=AF.Identity, scale=1.0),
                     waits=[m_dmm, free.get(("fst", tt))], inc=sA.inc())
                free[("dbk", tt)] = sA.at()
                qS.group()
                qS.dma(lambda e, th=th, mg=mg, tt=tt: e.dma_start(
                       out=s_ffn[th * 1024 + tt * 128: th * 1024 + (tt + 1) * 128, mg * 512:(mg + 1) * 512],
                       in_=a_fst[tt]),
                       waits=[sA.at()])
                if "ffn" in taps:
                    qS.dma(lambda e, th=th, mg=mg, tt=tt: e.dma_start(
                           out=taps["ffn"][th * 1024 + tt * 128: th * 1024 + (tt + 1) * 128,
                                           mg * 512:(mg + 1) * 512], in_=a_fst[tt]))
                free[("fst", tt)] = qS.done()
        free["hwin"] = sP.at()
        if th == 0:
            th0_done = [sP.at(), sA.at(), qS.all_done()]
            emit_G_half(0, th0_done)
    m_down_pe = sP.at()
    m_down_sync = qS.all_done()
    emit_G_half(1, [m_down_pe, sA.at(), m_down_sync])


    P.op("sync", lambda e: e.nop(), waits=[qS.all_done(), qA.all_done(), qG.all_done()])

    with es, nc.Block() as block:
        block.sync(lambda e: P.replay("sync", e))
        block.scalar(lambda e: P.replay("scalar", e))
        block.vector(lambda e: P.replay("vector", e))
        block.tensor(lambda e: P.replay("tensor", e))
        block.gpsimd(lambda e: P.replay("gpsimd", e))
    return nc


def _rope_perm():
    p = np.empty(128, np.int64)
    p[0:64] = np.arange(64) * 2
    p[64:128] = np.arange(64) * 2 + 1
    return p


def _ffn_perm():
    ip = np.arange(II)
    return 4 * (ip % 2048) + (ip // 2048)


def _prep_inputs(inputs):
    x = np.asarray(inputs["L0_attn_normalized_resid_pre"], np.float32)
    WQ = np.asarray(inputs["L0_attn_W_Q"], np.float32)
    bQ = np.asarray(inputs["L0_attn_b_Q"], np.float32)
    WK = np.asarray(inputs["L0_attn_W_K"], np.float32)
    bK = np.asarray(inputs["L0_attn_b_K"], np.float32)
    WV = np.asarray(inputs["L0_attn_W_V"], np.float32)
    bV = np.asarray(inputs["L0_attn_b_V"], np.float32)
    WO = np.asarray(inputs["L0_attn_W_O"], np.float32)
    bO = np.asarray(inputs["L0_attn_b_O"], np.float32)
    wg = np.asarray(inputs["L0_ffn_w_gate"], np.float32)
    wu = np.asarray(inputs["L0_ffn_w_up"], np.float32)
    wd = np.asarray(inputs["L0_ffn_w_down"], np.float32)

    rp = _rope_perm()
    fp = _ffn_perm()

    inv = 10000.0 ** (-np.arange(64, dtype=np.float64) * 2.0 / 128.0)
    ang = np.outer(inv, np.arange(T, dtype=np.float64))
    cosv = np.cos(ang)
    sinv = np.sin(ang)
    ctab = np.concatenate([cosv, cosv], axis=0).astype(np.float32)
    stab = np.concatenate([-sinv, sinv], axis=0).astype(np.float32)

    swap = np.zeros((128, 128), np.float32)
    swap[np.arange(64), np.arange(64) + 64] = 1.0
    swap[np.arange(64) + 64, np.arange(64)] = 1.0

    maskt = np.zeros((128, 2048), np.float32)
    for i in range(4):
        pidx = np.arange(128)[:, None]
        cidx = np.arange(512)[None, :]
        maskt[:, i * 512:(i + 1) * 512] = (i * 128 + pidx <= cidx).astype(np.float32)
    maskt = maskt.astype(ml_dtypes.bfloat16)

    identm = np.eye(128, dtype=np.float32)

    wg_p = wg[:, fp]
    wu_p = wu[:, fp]
    wd_p = wd[fp, :]
    wg_t = np.ascontiguousarray(wg_p.reshape(16, 128, 64, 128).transpose(2, 1, 0, 3).reshape(64, 128, 2048))
    wu_t = np.ascontiguousarray(wu_p.reshape(16, 128, 64, 128).transpose(2, 1, 0, 3).reshape(64, 128, 2048))
    wd_t = np.ascontiguousarray(wd_p.reshape(64, 128, 2048))

    maps = []
    for core in range(8):
        b, half = core // 2, core % 2
        h0 = half * 8
        wq_c = WQ[h0:h0 + 8][:, :, rp]
        wk_c = WK[h0:h0 + 8][:, :, rp]
        wq_t = np.ascontiguousarray(wq_c.reshape(8, 16, 128, 128).transpose(0, 2, 1, 3).reshape(8, 128, 2048))
        wk_t = np.ascontiguousarray(wk_c.reshape(8, 16, 128, 128).transpose(0, 2, 1, 3).reshape(8, 128, 2048))
        wv_c = WV[h0:h0 + 8]
        wv_t = np.ascontiguousarray(wv_c.transpose(1, 0, 2).reshape(2048, 1024)
                                    .reshape(16, 128, 1024).transpose(1, 0, 2).reshape(128, 16 * 1024))
        wo_c = WO[h0:h0 + 8].reshape(1024, 2048)
        wo_t = np.ascontiguousarray(wo_c.reshape(8, 128, 2048).transpose(1, 0, 2).reshape(128, 8 * 2048))
        bqk_c = np.stack([bQ[h0 + i][rp] for i in range(8)] + [bK[h0 + i][rp] for i in range(8)],
                         axis=1).astype(np.float32)
        bv_c = bV[h0:h0 + 8].reshape(1, 1024).astype(np.float32)
        bo_c = (bO.reshape(1, D) if half == 0 else np.zeros((1, D), np.float32))
        maps.append({
            "x": np.ascontiguousarray(x[b]),
            "x_own": np.ascontiguousarray(x[b, half * 2048:(half + 1) * 2048]),
            "wq": wq_t, "wk": wk_t, "wv": wv_t, "wo": wo_t,
            "bqk": bqk_c, "bvrow": bv_c, "borow": bo_c,
            "wg": wg_t, "wu": wu_t, "wd": wd_t,
            "ctab": ctab, "stab": stab, "mask_t": maskt, "ident": identm,
            "swapm": swap.astype(ml_dtypes.bfloat16),
            "own0": np.array([[half * 2048, 2048 - half * 2048]], np.int32),
        })
    return maps


_NC_CACHE = {}


def kernel(**inputs):
    key = tuple(_DEBUG_TAPS)
    if key not in _NC_CACHE:
        _NC_CACHE[key] = build_nc()
    nc = _NC_CACHE[key]
    maps = _prep_inputs(inputs)
    res = run_bass_kernel_spmd(nc, maps, core_ids=list(range(8)))
    out = np.empty((4, T, D), np.float32)
    for c in range(8):
        b, half = c // 2, c % 2
        out[b, half * 2048:(half + 1) * 2048, :] = np.asarray(res.results[c]["out"])
    if _DEBUG_TAPS:
        kernel.last_results = res
    return out
